# version 08
# baseline (speedup 1.0000x reference)
import contextlib
import os
import numpy as np
import concourse.bass as bass
import concourse.mybir as mybir
from concourse.bass_utils import run_bass_kernel_spmd

F32 = mybir.dt.float32
BF16 = mybir.dt.bfloat16
U8 = mybir.dt.uint8
I32 = mybir.dt.int32
AF = mybir.ActivationFunctionType
ALU = mybir.AluOpType
AX = mybir.AxisListType.X

T = 1024
NCH = 8
NBLK = 4
OB = 512
HCN = 16
NH = 16
NHB = 4
NCORE = 8
LOG_DKSC = float(-0.5 * np.log(128.0))
EPS_LN = 1e-5
EPS_RMS = 1e-5

ENG = ("pe", "act", "dve", "syq", "gpq")
ENG2SEC = {"pe": "tensor", "act": "scalar", "dve": "vector",
           "syq": "sync", "gpq": "gpsimd"}
CENG = ("pe", "act", "dve")
PSUMRES = frozenset("p%d" % i for i in range(8))


class B:

    def __init__(self, nc, es):
        self.nc = nc
        self.es = es
        self.prog = {k: [] for k in ENG}
        self.sems = {k: es.enter_context(nc.semaphore("s_" + k)) for k in ENG}
        self.xsems = {}
        self.cnt = {k: 0 for k in ENG}
        self.hwm = {k: {} for k in ENG}
        self.res = {}

    def xsem(self, name):
        if name not in self.xsems:
            self.xsems[name] = self.es.enter_context(self.nc.semaphore(name))
        return self.xsems[name]

    def _wait(self, eng, semkey, val):
        if semkey in ("syq", "gpq"):
            val = self.cnt[semkey]
        if val <= 0 or self.hwm[eng].get(semkey, 0) >= val:
            return
        self.hwm[eng][semkey] = val
        s = self.sems.get(semkey) or self.xsems[semkey]
        self.prog[eng].append(lambda e, s=s, v=val: e.wait_ge(s, v))

    def op(self, eng, fn, reads=(), writes=(), waits=(), inc=None):
        if inc is None:
            inc = 16 if eng in ("syq", "gpq") else 1
        reads = tuple(reads)
        writes = tuple(writes)
        excl = tuple(r for r in reads if r in PSUMRES)
        if excl:
            reads = tuple(r for r in reads if r not in PSUMRES)
            writes = writes + excl
        need = list(waits)
        for r in reads:
            g = self.res.get(r)
            if g and g["w"] is not None:
                we = g["w"][0]
                if we != eng or eng != "pe":
                    need.append(g["w"])
        for w in writes:
            g = self.res.get(w)
            if g:
                if g["w"] is not None:
                    we = g["w"][0]
                    if we != eng or eng in ("act", "dve", "syq", "gpq"):
                        need.append(g["w"])
                for t in g["r"]:
                    if t[0] != eng or eng in ("syq", "gpq"):
                        need.append(t)
        for semkey, val in need:
            self._wait(eng, semkey, val)
        self.cnt[eng] += inc
        tag = (eng, self.cnt[eng])
        s = self.sems[eng]
        self.prog[eng].append(lambda e, fn=fn, s=s, inc=inc:
                              fn(e).then_inc(s, inc))
        for r in reads:
            self.res.setdefault(r, {"w": None, "r": []})["r"].append(tag)
        for w in writes:
            self.res[w] = {"w": tag, "r": []}
        return tag

    def mark_write(self, resname, semkey, val):
        self.res[resname] = {"w": (semkey, val), "r": []}

    def barrier(self):
        for g in self.res.values():
            if g["w"] is not None and g["w"][0] in CENG:
                g["w"] = None
            g["r"] = [t for t in g["r"] if t[0] not in CENG]

    def run_block(self):
        nc = self.nc
        prog = {k: list(v) for k, v in self.prog.items()}
        for k in ENG:
            self.prog[k] = []
        with nc.Block() as blk:
            for k in ENG:
                if not prog[k]:
                    continue
                dec = getattr(blk, ENG2SEC[k])
                def make(ops):
                    def body(eng):
                        for f in ops:
                            f(eng)
                    return body
                dec(make(prog[k]))
        self.barrier()


def pipeline(stages, n):
    maxlag = max(l for l, _ in stages)
    for it in range(n + maxlag):
        for lag, fn in stages:
            i = it - lag
            if 0 <= i < n:
                fn(i)


_MAXPH = int(os.environ.get("KMAXPH", "99"))
_P2N = int(os.environ.get("KP2N", "32"))
_NOCOLL = int(os.environ.get("KNOCOLL", "0"))
_STAGES = os.environ.get("KSTAGES", "ACGTBEKD")
_DBG = int(os.environ.get("KDBG", "0"))


def build_nc():
    nc = bass.Bass("TRN2", target_bir_lowering=False)

    xin = nc.dram_tensor("xin", [T, 2048], F32, kind="ExternalInput").ap()
    wdr = [nc.dram_tensor(nm, [2048, 2048], F32, kind="ExternalInput").ap()
           for nm in ("wf", "wi", "wq", "wo")]
    wns = nc.dram_tensor("wns", [256, 2048], F32, kind="ExternalInput").ap()
    biasd = nc.dram_tensor("biasr", [1, 2048], F32, kind="ExternalInput").ap()
    gwd = nc.dram_tensor("gwcol", [128, 16], F32, kind="ExternalInput").ap()
    identd = nc.dram_tensor("identin", [128, 128], BF16, kind="ExternalInput").ap()
    lcd = nc.dram_tensor("lcin", [128, 128], BF16, kind="ExternalInput").ap()
    u2d = nc.dram_tensor("u2in", [128, 2], BF16, kind="ExternalInput").ap()
    u64d = nc.dram_tensor("u64in", [128, 1], BF16, kind="ExternalInput").ap()
    maskd = nc.dram_tensor("maskin", [128, 128], U8, kind="ExternalInput").ap()
    ones1d = nc.dram_tensor("ones1in", [1, 128], BF16, kind="ExternalInput").ap()
    onescd = nc.dram_tensor("onescin", [128, 1], BF16, kind="ExternalInput").ap()
    sel8d = nc.dram_tensor("sel8in", [8, 1024], BF16, kind="ExternalInput").ap()
    pmaskd = nc.dram_tensor("pmaskin", [128, 8], F32, kind="ExternalInput").ap()
    pidxd = nc.dram_tensor("pidxin", [1, 8], I32, kind="ExternalInput").ap()
    pltrd = nc.dram_tensor("pltrin", [128, 8], F32, kind="ExternalInput").ap()
    cstd = nc.dram_tensor("cstin", [128, 4], F32, kind="ExternalInput").ap()
    outd = nc.dram_tensor("out", [T, 2048], F32, kind="ExternalOutput").ap()
    dbgd = {}
    if _DBG:
        for nm, shape, dt in (("dln", [128, 16384], BF16),
                              ("dkb", [128, 16384], BF16),
                              ("dosb", [128, 16384], BF16),
                              ("dhsb", [128, 2048], F32),
                              ("deD", [T, 2048], BF16),
                              ("dvD", [T, 2048], BF16),
                              ("drows", [8, 2048], BF16),
                              ("decols", [128, 256], F32),
                              ("dgcol", [128, 16], F32),
                              ("dzbf", [128, 128], BF16),
                              ("dA", [128, 128], BF16),
                              ("dqT", [128, 128], BF16),
                              ("dkT", [128, 128], BF16),
                              ("dx1", [128, 16384], F32)):
            dbgd[nm] = nc.dram_tensor(nm, shape, dt, kind="ExternalOutput").ap()

    eD = nc.dram_tensor("eD", [T, 2048], BF16).ap()
    vD = nc.dram_tensor("vD", [T, 2048], BF16).ap()
    gin = nc.dram_tensor("gin", [2048, 128], F32).ap()
    gin2 = nc.dram_tensor("gin2", [128, 16], F32).ap()
    goutb_h = [nc.dram_tensor("goutb%d" % i, [NCORE * 512, 128], F32)
               for i in range(4)]
    gout2_h = nc.dram_tensor("gout2", [NCORE * 128, 16], F32)
    ginq = nc.dram_tensor("ginq", [2048, 256], BF16).ap()
    goutq = nc.dram_tensor("goutq", [NCORE * 2048, 256], BF16).ap()
    goutb = [h.ap() for h in goutb_h]
    gout2 = gout2_h.ap()

    es = contextlib.ExitStack()
    with es:
        def sbt(name, shape, dt):
            return es.enter_context(nc.sbuf_tensor(name, shape, dt))

        ident = sbt("identb", [128, 128], BF16)
        lc = sbt("lcb", [128, 128], BF16)
        u2 = sbt("u2b", [128, 2], BF16)
        u64c = sbt("u64b", [128, 1], BF16)
        masku = sbt("maskb", [128, 128], U8)
        ones1 = sbt("ones1b", [1, 128], BF16)
        onesc = sbt("onescb", [128, 1], BF16)
        sel8 = sbt("sel8b", [8, 1024], BF16)
        gwcol = sbt("gwb", [128, 16], F32)
        pmask = sbt("pmb", [128, 8], F32)
        pidx = sbt("pib", [1, 8], I32)
        pltr = sbt("pltrb", [128, 8], F32)
        zbf = sbt("zbf", [128, 128], BF16)
        cstb = sbt("cstb", [128, 4], F32)
        nc.const_aps.register(F32, 1.0, cstb[:, 0:1])
        nc.const_aps.register(F32, LOG_DKSC, cstb[:, 1:2])
        nc.const_aps.register(F32, EPS_LN, cstb[:, 2:3])
        ln = sbt("ln", [128, HCN * 1024], BF16)
        osb = sbt("osb", [128, NH * 1024], BF16)
        wst = sbt("wst", [128, 2 * HCN * OB], BF16)
        rows8 = sbt("rows8", [8, 2048], BF16)
        ecols = sbt("ecols", [128, 256], F32)
        gcol = sbt("gcolb", [128, 16], F32)
        hsb = sbt("hsb", [128, NH * 128], F32)
        st1 = sbt("st1", [128, 2], F32)
        st2 = sbt("st2", [128, 2], F32)
        smean = sbt("smean", [128, 2], F32)
        svar = sbt("svar", [128, 2], F32)
        srstd = sbt("srstd", [128, 2], F32)
        sm2 = sbt("sm2", [128, 2], F32)

        es1 = contextlib.ExitStack()
        def sb1(name, shape, dt):
            return es1.enter_context(nc.sbuf_tensor(name, shape, dt, side="right"))
        wq32 = sb1("wq32", [128, 2 * 2048], F32)
        awb = sb1("awb", [128, 2048], F32)
        sgnb = sb1("sgnb", [128, 2048], BF16)
        mskq = sb1("mskqb", [128, 2048], BF16)
        qtb = sb1("qtb", [128, 2048], BF16)
        qstage = sb1("qstage", [128, HCN * 128], BF16)
        thr = sb1("thrb", [128, 1], F32)
        xts = sb1("xts", [128, 2 * 2048], F32)
        xsq = sb1("xsq", [128, 2048], F32)
        ltst = sb1("ltst", [128, 2 * 2048], BF16)
        pbank = [es.enter_context(nc.psum_tensor("pb%d" % i, [128, 512], F32))
                 for i in range(8)]

        rgH = es.enter_context(nc.gpsimd.register("rgH"))
        rgG = es.enter_context(nc.gpsimd.register("rgG"))
        b = B(nc, es)
        dump_src = {}

        def do_dump():
            if not _DBG:
                return
            for nm, (srcap, res) in list(dump_src.items()):
                dma("syq", dbgd[nm], srcap, reads=(res,))
            vv = b.cnt["syq"]
            b.prog["syq"].append(
                lambda e, s=b.sems["syq"], v=vv: e.wait_ge(s, v))
            b.run_block()
        for s in ("cc_q", "cc_h", "cc_g"):
            b.xsem(s)

        def dma(q, dst, src, reads=(), writes=()):
            return b.op(q, lambda e, d=dst, s=src: e.dma_start(d, s),
                        reads=reads, writes=writes)

        def mm(out_ap, lhsT, rhs, start, stop, reads=(), writes=()):
            return b.op("pe", lambda e, o=out_ap, l=lhsT, r=rhs, st=start,
                        sp=stop: e.matmul(o, l, r, start=st, stop=sp),
                        reads=reads, writes=writes)

        def wsl(idx, hc):
            s0 = (idx % 2) * HCN * OB
            return wst[:, s0 + hc * OB: s0 + (hc + 1) * OB]

        def issue_wload(idx):
            blk = idx % 4
            slot = "wslot%d" % (idx % 2)
            s0 = (idx % 2) * HCN * OB
            if idx < 16:
                src = wdr[idx // 4][:, blk * OB:(blk + 1) * OB].rearrange(
                    "(hc p) o -> p hc o", p=128)
                dst = wst[:, s0:s0 + HCN * OB].rearrange(
                    "p (hc o) -> p hc o", hc=HCN)
                dma("gpq", dst, src, writes=(slot,))
            else:
                full = wst[:, s0:s0 + HCN * OB].rearrange(
                    "p (ic o) -> p ic o", o=OB)
                for half in range(2):
                    rank = blk * 2 + half
                    src = goutq[rank * 2048:(rank + 1) * 2048, :].rearrange(
                        "(ic p) o -> p ic o", p=128)
                    dma("gpq", full[:, :, half * 256:(half + 1) * 256], src,
                        reads=("goutq",), writes=(slot,))

        _wload_next = [0]

        def ensure_load(upto):
            while _wload_next[0] <= min(upto, 19):
                issue_wload(_wload_next[0])
                _wload_next[0] += 1

        def proj_mms(idx, c, pk, pres):
            for hc in range(HCN):
                lnsl = ln[:, hc * 1024 + c * 128: hc * 1024 + (c + 1) * 128]
                mm(pk[:, :], lnsl, wsl(idx, hc), hc == 0, hc == HCN - 1,
                   reads=("wslot%d" % (idx % 2), "lnr"), writes=(pres,))

        def emit_coll(kind, ins, outs, semname, waits=()):
            s = b.xsem(semname)
            if _NOCOLL:
                b.prog["gpq"].append(
                    lambda e, s=s: e.engine_nop().then_inc(s, 1))
                return
            for semkey, val in waits:
                b._wait("gpq", semkey, val)
            def f(e, s=s, ins=ins, outs=outs, kind=kind):
                e.collective_compute(
                    kind, ALU.bypass, replica_groups=[list(range(NCORE))],
                    ins=ins, outs=outs).then_inc(s, 1)
            b.prog["gpq"].append(f)

        for dst, src in ((ident, identd), (lc, lcd), (u2, u2d), (u64c, u64d),
                         (masku, maskd), (ones1, ones1d), (onesc, onescd),
                         (sel8, sel8d), (gwcol, gwd), (pmask, pmaskd),
                         (pidx, pidxd), (pltr, pltrd), (cstb, cstd)):
            dma("syq", dst[:, :], src, writes=("consts",))
        for rc in range(2):
            dma("syq", wq32[:, rc * 2048:(rc + 1) * 2048],
                wns[rc * 128:(rc + 1) * 128, :], writes=("wq32_%d" % rc,))
        ensure_load(1)
        b.op("dve", lambda e: e.memset(zbf[:, :], 0.0), writes=("zbf",))
        b.op("dve", lambda e: e.memset(gcol[:, :], 0.0), writes=("gcol",))
        b.op("dve", lambda e: e.memset(hsb[:, :], 0.0),
             writes=("H0", "H1", "H2", "H3"))
        b.run_block()
        if _MAXPH <= 1:
            do_dump()
            return nc

        for rc in range(2):
            wq_ = wq32[:, rc * 2048:(rc + 1) * 2048]
            rrc = ("wq32_%d" % rc,)
            b.op("dve", lambda e, w=wq_: e.reduce_sum(
                thr[:, :], w, axis=AX, apply_absolute_value=True),
                reads=rrc, writes=("thr",))
            b.op("dve", lambda e: e.tensor_scalar_mul(
                thr[:, :], thr[:, :], 0.7 / 2048.0),
                reads=("thr",), writes=("thr",))
            b.op("act", lambda e, w=wq_: e.activation(awb[:, :], w, AF.Abs),
                 reads=rrc, writes=("awq",))
            b.op("act", lambda e, w=wq_: e.activation(sgnb[:, :], w, AF.Sign),
                 reads=rrc, writes=("sgnq",))
            b.op("dve", lambda e: e.tensor_single_scalar(
                mskq[:, :], awb[:, :], thr[:, :], ALU.is_gt),
                reads=("awq", "thr"), writes=("mskq",))
            b.op("dve", lambda e: e.tensor_tensor(
                qtb[:, :], sgnb[:, :], mskq[:, :], ALU.mult),
                reads=("sgnq", "mskq"), writes=("qtb",))
            for ic in range(HCN):
                pk = pbank[6 + (ic % 2)]
                pr = "p%d" % (6 + ic % 2)
                mm(pk[:, 0:128], qtb[:, ic * 128:(ic + 1) * 128], ident[:, :],
                   True, True, reads=("qtb", "consts"), writes=(pr,))
                b.op("dve", lambda e, pk=pk, ic=ic: e.tensor_copy(
                    qstage[:, ic * 128:(ic + 1) * 128], pk[:, 0:128]),
                    reads=(pr,), writes=("qstage",))
            dma("syq",
                ginq[:, rc * 128:(rc + 1) * 128].rearrange(
                    "(ic p) o -> p ic o", p=128),
                qstage.rearrange("p (ic o) -> p ic o", o=128),
                reads=("qstage",), writes=("ginq",))

        def emit_ln(t, src_dram, tag):
            par = t % 2
            xq = xsq[:, 0:2048]
            lt = ltst[:, par * 2048:(par + 1) * 2048]
            c1 = st1[:, par:par + 1]
            c2 = st2[:, par:par + 1]
            cm = smean[:, par:par + 1]
            cv = svar[:, par:par + 1]
            cr = srstd[:, par:par + 1]
            c5 = sm2[:, par:par + 1]
            R1, R2, RM, RV, RR, R5 = ("st1%d" % par, "st2%d" % par,
                                      "smean%d" % par, "svar%d" % par,
                                      "srstd%d" % par, "sm2%d" % par)
            if src_dram is not None:
                xt = xts[:, par * 2048:(par + 1) * 2048]
                xr = ("xts%d" % par,)
                dma("syq", xt, src_dram[t * 128:(t + 1) * 128, :], writes=xr)
            else:
                xt = x1[:, t * 2048:(t + 1) * 2048]
                xr = ("x1_%d" % t,)
            b.op("dve", lambda e, xt=xt, c1=c1: e.reduce_sum(c1, xt, axis=AX),
                 reads=xr, writes=(R1,))
            b.op("dve", lambda e, xt=xt, xq=xq: e.tensor_tensor(
                xq, xt, xt, ALU.mult), reads=xr, writes=("xsq",))
            b.op("dve", lambda e, xq=xq, c2=c2: e.reduce_sum(c2, xq, axis=AX),
                 reads=("xsq",), writes=(R2,))
            b.op("dve", lambda e, c1=c1, cm=cm: e.tensor_scalar_mul(
                cm, c1, 1.0 / 2048.0), reads=(R1,), writes=(RM,))
            b.op("dve", lambda e, c2=c2, cv=cv: e.tensor_scalar_mul(
                cv, c2, 1.0 / 2048.0), reads=(R2,), writes=(RV,))
            b.op("dve", lambda e, cm=cm, c5=c5: e.tensor_tensor(
                c5, cm, cm, ALU.mult), reads=(RM,), writes=(R5,))
            b.op("dve", lambda e, cv=cv, c5=c5: e.tensor_tensor(
                cv, cv, c5, ALU.subtract), reads=(RV, R5), writes=(RV,))
            b.op("act", lambda e, cv=cv: e.activation(
                cv, cv, AF.Ln, bias=EPS_LN), reads=(RV,), writes=(RV,))
            b.op("act", lambda e, cv=cv, cr=cr: e.activation(
                cr, cv, AF.Exp, scale=-0.5), reads=(RV,), writes=(RR,))
            b.op("dve", lambda e, xt=xt, lt=lt, cm=cm, cr=cr: e.tensor_scalar(
                lt, xt, cm, cr, ALU.subtract, ALU.mult),
                reads=xr + (RM, RR), writes=("ltst%d" % par,))
            for hc in range(HCN):
                pk = pbank[6 + (hc % 2)]
                pr = "p%d" % (6 + hc % 2)
                mm(pk[:, 0:128], lt[:, hc * 128:(hc + 1) * 128], ident[:, :],
                   True, True, reads=("ltst%d" % par, "consts"), writes=(pr,))
                b.op("dve", lambda e, pk=pk, hc=hc, t=t: e.tensor_copy(
                    ln[:, hc * 1024 + t * 128: hc * 1024 + (t + 1) * 128],
                    pk[:, 0:128]), reads=(pr,), writes=("lnr",))

        for t in range(NCH):
            emit_ln(t, xin, "lnr")
        if _DBG:
            dump_src["dln"] = (ln[:, :], "lnr")
        b.run_block()
        if _MAXPH <= 2:
            do_dump()
            return nc

        es1.close()
        es2 = contextlib.ExitStack()
        def sb2(name, shape, dt):
            return es2.enter_context(nc.sbuf_tensor(name, shape, dt, side="right"))
        kb = sb2("kbt", [128, NCH * 2048], BF16)
        es2a = contextlib.ExitStack()
        def sb2a(name, shape, dt):
            return es2a.enter_context(nc.sbuf_tensor(name, shape, dt, side="right"))
        ta = sb2a("ta", [128, 3 * OB], F32)
        te = sb2a("te", [128, 3 * OB], F32)
        tf = sb2a("tf", [128, 3 * OB], BF16)
        fmn = sb2a("fmn", [128, 3 * OB], F32)
        g32 = sb2a("g32", [128, 3 * OB], F32)
        ghi = sb2a("ghi", [128, 3 * OB], BF16)
        glo = sb2a("glo", [128, 3 * OB], BF16)
        est = sb2a("est", [128, 3 * OB], BF16)
        srst = sb2a("srst", [1, 3 * OB], BF16)

        emit_coll("AllGather", [ginq.opt()], [goutq.opt()], "cc_q",
                  waits=[("syq", b.cnt["syq"])])
        b.mark_write("goutq", "cc_q", 1)

        FB = (0, 1, 6)
        GB = (2, 3, 7)

        def p2_A(i):
            blk, c = divmod(i, NCH)
            if c == 0:
                ensure_load(blk)
            proj_mms(blk, c, pbank[FB[i % 3]], "p%d" % FB[i % 3])
            if c == NCH - 1:
                ensure_load(blk + 1)

        def p2_C(i):
            par = i % 3
            fres = "p%d" % FB[par]
            taa = ta[:, par * OB:(par + 1) * OB]
            fp = pbank[FB[par]]
            b.op("act", lambda e, fp=fp, taa=taa: e.activation(
                taa, fp[:, :], AF.Abs), reads=(fres,), writes=("ta%d" % par,))
            b.op("act", lambda e, taa=taa: e.activation(
                taa, taa, AF.Exp, scale=-1.0),
                reads=("ta%d" % par,), writes=("ta%d" % par,))
            b.op("act", lambda e, taa=taa: e.activation(
                taa, taa, AF.Ln, bias=1.0),
                reads=("ta%d" % par,), writes=("ta%d" % par,))

        def p2_G(i):
            par = i % 3
            fres = "p%d" % FB[par]
            fp = pbank[FB[par]]
            fmm = fmn[:, par * OB:(par + 1) * OB]
            gg = g32[:, par * OB:(par + 1) * OB]
            gh = ghi[:, par * OB:(par + 1) * OB]
            gl = glo[:, par * OB:(par + 1) * OB]
            taa = ta[:, par * OB:(par + 1) * OB]
            b.op("dve", lambda e, fp=fp, fmm=fmm: e.tensor_scalar_min(
                fmm, fp[:, :], 0.0), reads=(fres,), writes=("fmn%d" % par,))
            b.op("dve", lambda e, fmm=fmm, gg=gg, taa=taa: e.tensor_tensor(
                gg, fmm, taa, ALU.subtract),
                reads=("fmn%d" % par, "ta%d" % par), writes=("g%d" % par,))
            b.op("dve", lambda e, gg=gg, gh=gh: e.tensor_copy(gh, gg),
                 reads=("g%d" % par,), writes=("gh%d" % par,))
            b.op("dve", lambda e, gg=gg, gh=gh, gl=gl: e.tensor_tensor(
                gl, gg, gh, ALU.subtract),
                reads=("g%d" % par, "gh%d" % par), writes=("gl%d" % par,))

        def p2_T(i):
            par = i % 3
            tee = te[:, par * OB:(par + 1) * OB]
            gg = g32[:, par * OB:(par + 1) * OB]
            b.op("act", lambda e, gg=gg, tee=tee: e.activation(
                tee, gg, AF.Exp), reads=("g%d" % par,), writes=("te%d" % par,))

        def p2_B(i):
            par = i % 3
            gh = ghi[:, par * OB:(par + 1) * OB]
            gl = glo[:, par * OB:(par + 1) * OB]
            gp = pbank[GB[par]]
            gres = "p%d" % GB[par]
            mm(gp[:, :], lc[:, :], gh, True, False,
               reads=("gh%d" % par, "consts"), writes=(gres,))
            mm(gp[:, :], lc[:, :], gl, False, True,
               reads=("gl%d" % par,), writes=(gres,))
            rp = pbank[4]
            mm(rp[0:1, :], u64c[:, :], gh, True, False,
               reads=("gh%d" % par,), writes=("p4",))
            mm(rp[0:1, :], u64c[:, :], gl, False, True,
               reads=("gl%d" % par,), writes=("p4",))
            cp = pbank[5]
            for hl in range(NHB):
                mm(cp[:, hl * 2: hl * 2 + 2],
                   gh[:, hl * 128:(hl + 1) * 128], u2[:, :], True, False,
                   reads=("gh%d" % par,), writes=("p5",))
                mm(cp[:, hl * 2: hl * 2 + 2],
                   gl[:, hl * 128:(hl + 1) * 128], u2[:, :], False, True,
                   reads=("gl%d" % par,), writes=("p5",))

        def p2_E(i):
            blk, c = divmod(i, NCH)
            par = i % 3
            gp = pbank[GB[par]]
            gres = "p%d" % GB[par]
            ee = est[:, par * OB:(par + 1) * OB]
            tff = tf[:, par * OB:(par + 1) * OB]
            sr = srst[0:1, par * OB:(par + 1) * OB]
            ecsl = ecols[:, (c * 16 + blk * 4) * 2: (c * 16 + blk * 4) * 2 + 8]
            b.op("act", lambda e, gp=gp, ee=ee: e.activation(
                ee, gp[:, :], AF.Exp, bias=LOG_DKSC),
                reads=(gres,), writes=("est%d" % par,))
            b.op("act", lambda e, gp=gp, tff=tff: e.activation(
                tff, gp[:, :], AF.Exp, scale=-1.0),
                reads=(gres,), writes=("tf%d" % par,))
            b.op("act", lambda e, sr=sr: e.activation(
                sr, pbank[4][0:1, :], AF.Exp),
                reads=("p4",), writes=("sr%d" % par,))
            b.op("act", lambda e, ecsl=ecsl: e.activation(
                ecsl, pbank[5][:, 0:8], AF.Exp),
                reads=("p5",), writes=("ecols",))

        def p2_K(i):
            blk, c = divmod(i, NCH)
            par = i % 3
            tee = te[:, par * OB:(par + 1) * OB]
            tff = tf[:, par * OB:(par + 1) * OB]
            kbsl = kb[:, c * 2048 + blk * OB: c * 2048 + (blk + 1) * OB]
            b.op("dve", lambda e, tee=tee: e.tensor_scalar(
                tee, tee, -1.0, -1.0, ALU.add, ALU.mult),
                reads=("te%d" % par,), writes=("te%d" % par,))
            b.op("dve", lambda e, tee=tee, tff=tff, kbsl=kbsl: e.tensor_tensor(
                kbsl, tee, tff, ALU.mult),
                reads=("te%d" % par, "tf%d" % par), writes=("kbr",))
            gcsl = gcol[:, blk * 4:(blk + 1) * 4]
            cp = pbank[5]
            b.op("dve", lambda e, gcsl=gcsl, cp=cp: e.tensor_tensor(
                gcsl, gcsl, cp[:, 1:8:2], ALU.add),
                reads=("gcol", "p5"), writes=("gcol",))

        def p2_D(i):
            blk, c = divmod(i, NCH)
            par = i % 3
            ee = est[:, par * OB:(par + 1) * OB]
            sr = srst[0:1, par * OB:(par + 1) * OB]
            dma("syq", eD[c * 128:(c + 1) * 128, blk * OB:(blk + 1) * OB],
                ee, reads=("est%d" % par,), writes=("eDram",))
            dma("gpq", rows8[c:c + 1, blk * OB:(blk + 1) * OB], sr,
                reads=("sr%d" % par,), writes=("rows8",))

        _all = [("A", 0, p2_A), ("C", 1, p2_C), ("G", 1, p2_G),
                ("T", 2, p2_T), ("E", 3, p2_E), ("K", 3, p2_K),
                ("D", 3, p2_D), ("B", 2, p2_B)]
        pipeline([(l, f) for nm, l, f in _all if nm in _STAGES], _P2N)
        ensure_load(5)
        if _DBG:
            dump_src["dkb"] = (kb[:, :], "kbr")
            dump_src["deD"] = (eD, "eDram")
            dump_src["drows"] = (rows8[:, :], "rows8")
            dump_src["decols"] = (ecols[:, :], "ecols")
            dump_src["dgcol"] = (gcol[:, :], "gcol")
        b.run_block()
        if _MAXPH <= 3:
            do_dump()
            return nc

        es2a.close()
        es2b = contextlib.ExitStack()
        def sb2b(name, shape, dt):
            return es2b.enter_context(nc.sbuf_tensor(name, shape, dt, side="right"))
        vblk = sb2b("vblk", [128, 2 * NCH * OB], BF16)
        eblk = sb2b("eblk", [128, 2 * NCH * OB], BF16)
        kdst = sb2b("kdst", [128, 2 * OB], BF16)
        qst = sb2b("qstb", [128, 2 * OB], BF16)
        qbst = sb2b("qbst", [128, 2 * OB], BF16)
        qbT = sb2b("qbTb", [128, 2 * 128], BF16)
        kbT = sb2b("kbTb", [128, 2 * 128], BF16)
        asb = sb2b("asbb", [128, 2 * 128], BF16)
        hpsb = sb2b("hpsb", [128, 2 * 128], BF16)
        hp2 = sb2b("hp2", [128, 2 * 512], F32)
        g2all = sb2b("g2all", [128, 128], F32)
        sall = sb2b("sall", [128, 128], F32)
        smy = sb2b("smy", [128, 16], F32)
        sclall = sb2b("sclall", [128, 128], F32)
        tdif = sb2b("tdif", [128, 16], F32)

        dma("syq", gin2[:, :], gcol[:, :], reads=("gcol",), writes=("gin2",))
        emit_coll("AllGather", [gin2.opt()], [gout2.opt()], "cc_g",
                  waits=[("syq", b.cnt["syq"])])
        b.mark_write("gout2r", "cc_g", 1)

        def p3_A(i):
            blk, c = divmod(i, NCH)
            if c == 0:
                ensure_load(4 + blk)
            proj_mms(4 + blk, c, pbank[i % 2], "p%d" % (i % 2))
            if c == NCH - 1:
                ensure_load(4 + blk + 1)

        def p3_V(i):
            blk, c = divmod(i, NCH)
            par = i % 2
            ip = pbank[par]
            bp = blk % 2
            vsl = vblk[:, bp * NCH * OB + c * OB: bp * NCH * OB + (c + 1) * OB]
            b.op("dve", lambda e, ip=ip, vsl=vsl: e.tensor_copy(vsl, ip[:, :]),
                 reads=("p%d" % par,), writes=("vb%d_%d" % (bp, c),))

        def p3_S(i):
            blk, c = divmod(i, NCH)
            sp = pbank[2]
            mm(sp[:, :], sel8[:, c * 128:(c + 1) * 128],
               rows8[0:8, blk * OB:(blk + 1) * OB], True, True,
               reads=("rows8", "consts"), writes=("p2",))

        def p3_K(i):
            blk, c = divmod(i, NCH)
            par = i % 2
            kd = kdst[:, par * OB:(par + 1) * OB]
            kbsl = kb[:, c * 2048 + blk * OB: c * 2048 + (blk + 1) * OB]
            sp = pbank[2]
            b.op("dve", lambda e, kd=kd, kbsl=kbsl, sp=sp: e.tensor_tensor(
                kd, kbsl, sp[:, :], ALU.mult),
                reads=("kbr", "p2"), writes=("kd%d" % par,))

        def p3_W(i):
            blk, c = divmod(i, NCH)
            par = i % 2
            bp = blk % 2
            kd = kdst[:, par * OB:(par + 1) * OB]
            vsl = vblk[:, bp * NCH * OB + c * OB: bp * NCH * OB + (c + 1) * OB]
            for hl in range(NHB):
                hg = blk * 4 + hl
                hup = pbank[3]
                mm(hup[:, 0:128], kd[:, hl * 128:(hl + 1) * 128],
                   vsl[:, hl * 128:(hl + 1) * 128], True, True,
                   reads=("kd%d" % par, "vb%d_%d" % (bp, c)), writes=("p3",))
                hsl = hsb[:, hg * 128:(hg + 1) * 128]
                ecsl = ecols[:, (c * 16 + hg) * 2 + 1: (c * 16 + hg) * 2 + 2]
                b.op("dve", lambda e, hsl=hsl, ecsl=ecsl, hup=hup:
                     e.scalar_tensor_tensor(hsl, hsl, ecsl, hup[:, 0:128],
                                            ALU.mult, ALU.add),
                     reads=("H%d" % blk, "ecols", "p3"), writes=("H%d" % blk,))
            if c == NCH - 1:
                vfull = vblk[:, bp * NCH * OB:(bp + 1) * NCH * OB].rearrange(
                    "p (cc d) -> p cc d", d=OB)
                dma("syq", vD[:, blk * OB:(blk + 1) * OB].rearrange(
                    "(cc p) d -> p cc d", p=128), vfull,
                    reads=tuple("vb%d_%d" % (bp, cc) for cc in range(NCH)),
                    writes=("vDram%d" % blk,))
                dma("syq",
                    gin[blk * 512:(blk + 1) * 512, :].rearrange(
                        "(h d) v -> d h v", d=128),
                    hsb[:, blk * 512:(blk + 1) * 512].rearrange(
                        "p (h v) -> p h v", v=128),
                    reads=("H%d" % blk,), writes=("ginb%d" % blk,))
                emit_coll("AllGather", [gin[blk * 512:(blk + 1) * 512, :].opt()],
                          [goutb[blk].opt()], "cc_h%d" % blk,
                          waits=[("syq", b.cnt["syq"])])
                b.mark_write("goutr%d" % blk, "cc_h%d" % blk, 1)

        pipeline([(0, p3_A), (1, p3_V), (2, p3_K), (1, p3_S), (2, p3_W)],
                 NBLK * NCH)
        ensure_load(9)
        if _DBG:
            dump_src["dvD"] = (vD, "vDram3")
            dump_src["dhsb"] = (hsb[:, :], "H3")
        b.run_block()
        if _MAXPH <= 4:
            do_dump()
            return nc

        for blk in range(4):
            b.op("dve", lambda e, blk=blk: e.memset(
                hsb[:, blk * 512:(blk + 1) * 512], 0.0),
                reads=("ginb%d" % blk,), writes=("H%d" % blk,))
        dma("syq", g2all.rearrange("p (c h) -> p c h", h=16),
            gout2.rearrange("(c d) h -> d c h", d=128),
            reads=("gout2r",), writes=("g2all",))
        b.op("dve", lambda e: e.tensor_copy(sall[:, 0:16], g2all[:, 0:16]),
             reads=("g2all",), writes=("sall",))
        for p in range(1, 8):
            b.op("dve", lambda e, p=p: e.tensor_tensor(
                sall[:, p * 16:(p + 1) * 16], sall[:, (p - 1) * 16:p * 16],
                g2all[:, p * 16:(p + 1) * 16], ALU.add),
                reads=("sall", "g2all"), writes=("sall",))
        b.op("dve", lambda e: e.memset(smy[:, :], 0.0), writes=("smy",))
        for p in range(8):
            b.op("dve", lambda e, p=p: e.scalar_tensor_tensor(
                smy[:, :], g2all[:, p * 16:(p + 1) * 16], pltr[:, p:p + 1],
                smy[:, :], ALU.mult, ALU.add),
                reads=("g2all", "consts", "smy"), writes=("smy",))
        for p in range(8):
            b.op("dve", lambda e, p=p: e.tensor_tensor(
                tdif[:, :], smy[:, :], sall[:, p * 16:(p + 1) * 16],
                ALU.subtract), reads=("smy", "sall"), writes=("tdif",))
            b.op("dve", lambda e, p=p: e.tensor_single_scalar(
                tdif[:, :], tdif[:, :], pltr[:, p:p + 1], ALU.mult),
                reads=("tdif", "consts"), writes=("tdif",))
            b.op("act", lambda e, p=p: e.activation(
                sclall[:, p * 16:(p + 1) * 16], tdif[:, :], AF.Exp),
                reads=("tdif",), writes=("sclall",))
            b.op("dve", lambda e, p=p: e.tensor_single_scalar(
                sclall[:, p * 16:(p + 1) * 16],
                sclall[:, p * 16:(p + 1) * 16], pltr[:, p:p + 1], ALU.mult),
                reads=("sclall", "consts"), writes=("sclall",))

        def emit_hin(blk):
            for p in range(8):
                pp = p % 2
                dma("syq", hp2[:, pp * 512:(pp + 1) * 512].rearrange(
                    "p (h v) -> p h v", v=128),
                    goutb[blk][p * 512:(p + 1) * 512, :].rearrange(
                        "(h d) v -> d h v", d=128),
                    reads=("goutr%d" % blk,), writes=("hpb%d" % pp,))
                for hl in range(4):
                    hg = blk * 4 + hl
                    hsl = hsb[:, hg * 128:(hg + 1) * 128]
                    b.op("dve", lambda e, hsl=hsl, hl=hl, pp=pp, p=p, hg=hg:
                         e.scalar_tensor_tensor(
                             hsl,
                             hp2[:, pp * 512 + hl * 128:
                                 pp * 512 + (hl + 1) * 128],
                             sclall[:, p * 16 + hg: p * 16 + hg + 1],
                             hsl, ALU.mult, ALU.add),
                         reads=("hpb%d" % pp, "sclall", "H%d" % blk),
                         writes=("H%d" % blk,))

        def p4_load(blk):
            bp = blk % 2
            dma("syq", eblk[:, bp * NCH * OB:(bp + 1) * NCH * OB].rearrange(
                "p (cc d) -> p cc d", d=OB),
                eD[:, blk * OB:(blk + 1) * OB].rearrange(
                    "(cc p) d -> p cc d", p=128),
                reads=("eDram",), writes=("eblk%d" % bp,))
            dma("syq", vblk[:, bp * NCH * OB:(bp + 1) * NCH * OB].rearrange(
                "p (cc d) -> p cc d", d=OB),
                vD[:, blk * OB:(blk + 1) * OB].rearrange(
                    "(cc p) d -> p cc d", p=128),
                reads=("vDram%d" % blk,), writes=("vbl%d" % bp,))

        p4_load(0)
        p4_load(1)

        def p4_A(i):
            blk, c = divmod(i, NCH)
            if c == 0:
                ensure_load(8 + blk)
                emit_hin(blk)
            proj_mms(8 + blk, c, pbank[i % 2], "p%d" % (i % 2))
            if c == NCH - 1:
                ensure_load(8 + blk + 1)

        def p4_S(i):
            par = i % 2
            qp = pbank[par]
            qs = qst[:, par * OB:(par + 1) * OB]
            b.op("act", lambda e, qp=qp, qs=qs: e.activation(
                qs, qp[:, :], AF.Silu),
                reads=("p%d" % par,), writes=("qs%d" % par,))

        def p4_Q(i):
            blk, c = divmod(i, NCH)
            par = i % 2
            bp = blk % 2
            qs = qst[:, par * OB:(par + 1) * OB]
            qbs = qbst[:, par * OB:(par + 1) * OB]
            eb = eblk[:, bp * NCH * OB + c * OB: bp * NCH * OB + (c + 1) * OB]
            b.op("dve", lambda e, qbs=qbs, qs=qs, eb=eb: e.tensor_tensor(
                qbs, qs, eb, ALU.mult),
                reads=("qs%d" % par, "eblk%d" % bp), writes=("qb%d" % par,))

        def p4_SB(i):
            blk, c = divmod(i, NCH)
            sp = pbank[6]
            mm(sp[:, :], sel8[:, c * 128:(c + 1) * 128],
               rows8[0:8, blk * OB:(blk + 1) * OB], True, True,
               reads=("rows8", "consts"), writes=("p6",))

        def p4_K(i):
            blk, c = divmod(i, NCH)
            par = i % 2
            kd = kdst[:, par * OB:(par + 1) * OB]
            kbsl = kb[:, c * 2048 + blk * OB: c * 2048 + (blk + 1) * OB]
            sp = pbank[6]
            b.op("dve", lambda e, kd=kd, kbsl=kbsl, sp=sp: e.tensor_tensor(
                kd, kbsl, sp[:, :], ALU.mult),
                reads=("kbr", "p6"), writes=("kd%d" % par,))

        def p4_W(i):
            blk, c = divmod(i, NCH)
            par = i % 2
            bp = blk % 2
            qbs = qbst[:, par * OB:(par + 1) * OB]
            kbsl = kb[:, c * 2048 + blk * OB: c * 2048 + (blk + 1) * OB]
            kd = kdst[:, par * OB:(par + 1) * OB]
            for hl in range(NHB):
                hg = blk * 4 + hl
                hpar = hl % 2
                vsl = vblk[:, bp * NCH * OB + c * OB + hl * 128:
                           bp * NCH * OB + c * OB + (hl + 1) * 128]
                qT = qbT[:, hpar * 128:(hpar + 1) * 128]
                kT = kbT[:, hpar * 128:(hpar + 1) * 128]
                am = asb[:, hpar * 128:(hpar + 1) * 128]
                hp = hpsb[:, hpar * 128:(hpar + 1) * 128]
                hsl = hsb[:, hg * 128:(hg + 1) * 128]
                ecg = ecols[:, (c * 16 + hg) * 2: (c * 16 + hg) * 2 + 1]
                ect = ecols[:, (c * 16 + hg) * 2 + 1: (c * 16 + hg) * 2 + 2]
                mm(pbank[2][:, 0:128], qbs[:, hl * 128:(hl + 1) * 128],
                   ident[:, :], True, True,
                   reads=("qb%d" % par,), writes=("p2",))
                mm(pbank[3][:, 0:128], kbsl[:, hl * 128:(hl + 1) * 128],
                   ident[:, :], True, True, reads=("kbr",), writes=("p3",))
                b.op("dve", lambda e, qT=qT: e.tensor_copy(
                    qT, pbank[2][:, 0:128]), reads=("p2",),
                    writes=("qT%d" % hpar,))
                b.op("dve", lambda e, kT=kT: e.tensor_copy(
                    kT, pbank[3][:, 0:128]), reads=("p3",),
                    writes=("kT%d" % hpar,))
                b.op("dve", lambda e, hp=hp, hsl=hsl, ecg=ecg:
                     e.tensor_single_scalar(hp, hsl, ecg, ALU.mult),
                     reads=("H%d" % blk, "ecols"), writes=("hpp%d" % hpar,))
                mm(pbank[4][:, 0:128], kT, qT, True, True,
                   reads=("kT%d" % hpar, "qT%d" % hpar), writes=("p4",))
                b.op("dve", lambda e, am=am: e.select(
                    am, masku[:, :], pbank[4][:, 0:128], zbf[:, :]),
                    reads=("p4", "consts", "zbf"), writes=("am%d" % hpar,))
                mm(pbank[5][:, 0:128], vsl, am, True, False,
                   reads=("am%d" % hpar, "vbl%d" % bp), writes=("p5",))
                mm(pbank[5][:, 0:128], hp, qT, False, True,
                   reads=("hpp%d" % hpar, "qT%d" % hpar), writes=("p5",))
                mm(pbank[7][:, 0:128], kd[:, hl * 128:(hl + 1) * 128], vsl,
                   True, True,
                   reads=("kd%d" % par, "vbl%d" % bp), writes=("p7",))
                osl = osb[:, hg * 1024 + c * 128: hg * 1024 + (c + 1) * 128]
                b.op("dve", lambda e, osl=osl: e.tensor_copy(
                    osl, pbank[5][:, 0:128]), reads=("p5", "p7"),
                    writes=("osbr",))
                b.op("dve", lambda e, hsl=hsl, ect=ect: e.scalar_tensor_tensor(
                    hsl, hsl, ect, pbank[7][:, 0:128], ALU.mult, ALU.add),
                    reads=("H%d" % blk, "ecols", "p7"), writes=("H%d" % blk,))
            if c == NCH - 1 and blk + 2 < NBLK:
                p4_load(blk + 2)

        pipeline([(0, p4_A), (1, p4_S), (2, p4_Q), (2, p4_K), (1, p4_SB),
                  (3, p4_W)], NBLK * NCH)
        if _DBG:
            dump_src["dosb"] = (osb[:, :], "osbr")
            dump_src["dzbf"] = (zbf[:, :], "zbf")
        issue_wload(12)
        issue_wload(13)
        b.run_block()
        if _MAXPH <= 6:
            do_dump()
            return nc

        es2b.close()
        es2.close()
        es3 = contextlib.ExitStack()
        def sb3(name, shape, dt):
            return es3.enter_context(nc.sbuf_tensor(name, shape, dt, side="right"))
        x1 = sb3("x1b", [128, NCH * 2048], F32)
        biasB = sb3("biasB", [128, 2048], F32)
        es3a = contextlib.ExitStack()
        def sb3a(name, shape, dt):
            return es3a.enter_context(nc.sbuf_tensor(name, shape, dt, side="right"))
        xts = sb3a("xts2", [128, 2048], F32)
        o2st = sb3a("o2st", [128, 1024], BF16)
        rstdrow = sb3a("rstdrow", [1, 1024], F32)
        rhirow = sb3a("rhirow", [1, 1024], BF16)
        rlorow = sb3a("rlorow", [1, 1024], BF16)
        rstdB = sb3a("rstdB", [128, 1024], F32)

        for h in range(NH):
            o2 = o2st[:, 0:1024]
            b.op("act", lambda e, o2=o2, h=h: e.activation(
                o2, osb[:, h * 1024:(h + 1) * 1024], AF.Square),
                reads=("osbr",), writes=("o2s",))
            for half in range(2):
                pk = pbank[half]
                mm(pk[0:1, :], onesc[:, :],
                   o2[:, half * 512:(half + 1) * 512], h == 0, h == NH - 1,
                   reads=("o2s", "consts"), writes=("p%d" % half,))
        for half in range(2):
            pk = pbank[half]
            rr = rstdrow[0:1, half * 512:(half + 1) * 512]
            b.op("act", lambda e, rr=rr, pk=pk: e.activation(
                rr, pk[0:1, :], AF.Ln, bias=EPS_RMS, scale=1.0 / 2048.0),
                reads=("p%d" % half,), writes=("rsr",))
            b.op("act", lambda e, rr=rr: e.activation(
                rr, rr, AF.Exp, scale=-0.5), reads=("rsr",), writes=("rsr",))
        b.op("dve", lambda e: e.tensor_copy(rhirow[0:1, :], rstdrow[0:1, :]),
             reads=("rsr",), writes=("rhi",))
        b.op("dve", lambda e: e.tensor_tensor(
            rlorow[0:1, :], rstdrow[0:1, :], rhirow[0:1, :], ALU.subtract),
            reads=("rsr", "rhi"), writes=("rlo",))
        for half in range(2):
            pk = pbank[2 + half]
            pr = "p%d" % (2 + half)
            mm(pk[:, :], ones1[:, :], rhirow[0:1, half * 512:(half + 1) * 512],
               True, False, reads=("rhi", "consts"), writes=(pr,))
            mm(pk[:, :], ones1[:, :], rlorow[0:1, half * 512:(half + 1) * 512],
               False, True, reads=("rlo",), writes=(pr,))
            b.op("dve", lambda e, pk=pk, half=half: e.tensor_copy(
                rstdB[:, half * 512:(half + 1) * 512], pk[:, :]),
                reads=(pr,), writes=("rstdB",))
        for h in range(NH):
            osl = osb[:, h * 1024:(h + 1) * 1024]
            b.op("dve", lambda e, osl=osl: e.tensor_tensor(
                osl, osl, rstdB[:, :], ALU.mult),
                reads=("osbr", "rstdB"), writes=("osbr",))
            b.op("dve", lambda e, osl=osl, h=h: e.tensor_single_scalar(
                osl, osl, gwcol[:, h:h + 1], ALU.mult),
                reads=("osbr", "consts"), writes=("osbr",))
        for blk in range(NBLK):
            idx = 12 + blk
            ensure_load(12 + blk)
            for t in range(NCH):
                j = blk * NCH + t
                par = j % 2
                pw = pbank[4 + par]
                pres = "p%d" % (4 + par)
                for r in range(HCN):
                    mm(pw[:, :],
                       osb[:, r * 1024 + t * 128: r * 1024 + (t + 1) * 128],
                       wsl(idx, r), r == 0, r == HCN - 1,
                       reads=("wslot%d" % (idx % 2), "osbr"), writes=(pres,))
                x1sl = x1[:, t * 2048 + blk * OB: t * 2048 + (blk + 1) * OB]
                b.op("dve", lambda e, x1sl=x1sl, pw=pw: e.tensor_copy(
                    x1sl, pw[:, :]), reads=(pres,), writes=("x1_%d" % t,))
            ensure_load(12 + blk + 1)
        for t in range(NCH):
            xt = xts[:, 0:2048]
            dma("syq", xt, xin[t * 128:(t + 1) * 128, :], writes=("xtsr",))
            x1sl = x1[:, t * 2048:(t + 1) * 2048]
            b.op("dve", lambda e, x1sl=x1sl, xt=xt: e.tensor_tensor(
                x1sl, x1sl, xt, ALU.add),
                reads=("x1_%d" % t, "xtsr"), writes=("x1_%d" % t,))
        if _DBG:
            dump_src["dx1"] = (x1[:, :], "x1_7")
        ensure_load(17)
        b.run_block()
        if _MAXPH <= 7:
            do_dump()
            return nc

        es3a.close()
        es3b0 = contextlib.ExitStack()
        def sb3b0(name, shape, dt):
            return es3b0.enter_context(nc.sbuf_tensor(name, shape, dt, side="right"))
        biasrow = sb3b0("brb", [1, 2048], F32)
        bhirow = sb3b0("bhirow", [1, 2048], BF16)
        blorow = sb3b0("blorow", [1, 2048], BF16)
        dma("syq", biasrow[:, :], biasd, writes=("biasrow",))

        b.op("dve", lambda e: e.tensor_copy(bhirow[0:1, :], biasrow[0:1, :]),
             reads=("biasrow",), writes=("bhi",))
        b.op("dve", lambda e: e.tensor_tensor(
            blorow[0:1, :], biasrow[0:1, :], bhirow[0:1, :], ALU.subtract),
            reads=("biasrow", "bhi"), writes=("blo",))
        for q in range(4):
            pk = pbank[q % 2]
            pr = "p%d" % (q % 2)
            mm(pk[:, :], ones1[:, :], bhirow[0:1, q * 512:(q + 1) * 512],
               True, False, reads=("bhi", "consts"), writes=(pr,))
            mm(pk[:, :], ones1[:, :], blorow[0:1, q * 512:(q + 1) * 512],
               False, True, reads=("blo",), writes=(pr,))
            b.op("dve", lambda e, pk=pk, q=q: e.tensor_copy(
                biasB[:, q * 512:(q + 1) * 512], pk[:, :]),
                reads=(pr,), writes=("biasB",))
        b.run_block()
        if _MAXPH <= 8:
            do_dump()
            return nc
        es3b0.close()
        es3b = contextlib.ExitStack()
        def sb3b(name, shape, dt):
            return es3b.enter_context(nc.sbuf_tensor(name, shape, dt, side="right"))
        xsq = sb3b("xsq2", [128, 2048], F32)
        ltst = sb3b("ltst2", [128, 2 * 2048], BF16)
        outst = sb3b("outst", [128, 2 * OB], F32)

        for t in range(NCH):
            emit_ln(t, None, "lnr")
        b.run_block()
        if _MAXPH <= 9:
            do_dump()
            return nc

        for blk in range(NBLK):
            idx = 16 + blk
            ensure_load(16 + blk)
            for t in range(NCH):
                j = blk * NCH + t
                par = j % 2
                pf = pbank[2 + par]
                pres = "p%d" % (2 + par)
                for ic in range(HCN):
                    mm(pf[:, :],
                       ln[:, ic * 1024 + t * 128: ic * 1024 + (t + 1) * 128],
                       wsl(idx, ic), ic == 0, ic == HCN - 1,
                       reads=("wslot%d" % (idx % 2), "lnr"), writes=(pres,))
                ost = outst[:, par * OB:(par + 1) * OB]
                b.op("dve", lambda e, ost=ost, pf=pf, blk=blk:
                     e.scalar_tensor_tensor(
                         ost, pf[:, :], 1.0,
                         biasB[:, blk * OB:(blk + 1) * OB], ALU.mult, ALU.add),
                     reads=(pres, "biasB"), writes=("ost%d" % par,))
                b.op("dve", lambda e, ost=ost, t=t, blk=blk: e.tensor_tensor(
                    ost, ost,
                    x1[:, t * 2048 + blk * OB: t * 2048 + (blk + 1) * OB],
                    ALU.add),
                    reads=("ost%d" % par, "x1_%d" % t), writes=("ost%d" % par,))
                dma("syq",
                    outd[t * 128:(t + 1) * 128, blk * OB:(blk + 1) * OB],
                    ost, reads=("ost%d" % par,))
            ensure_load(16 + blk + 1)
        vfin = b.cnt["syq"]
        b.prog["syq"].append(
            lambda e, s=b.sems["syq"], v=vfin: e.wait_ge(s, v))
        b.run_block()
        if _MAXPH <= 99:
            do_dump()
        es3b.close()
        es3.close()

    return nc


_NC_CACHE = {}
_LAST_RES = None


def _get_nc():
    if "nc" not in _NC_CACHE:
        _NC_CACHE["nc"] = build_nc()
    return _NC_CACHE["nc"]


def kernel(L0_attn_x, L0_attn_Wq, L0_attn_Wf, L0_attn_Wi, L0_attn_Wo,
           L0_attn_g_weight, L0_ffn_weight, L0_ffn_bias):
    import ml_dtypes
    BFD = ml_dtypes.bfloat16
    x = np.ascontiguousarray(
        np.asarray(L0_attn_x, dtype=np.float32).reshape(8192, 2048))
    Wq = np.ascontiguousarray(np.asarray(L0_attn_Wq, dtype=np.float32))
    Wf = np.ascontiguousarray(np.asarray(L0_attn_Wf, dtype=np.float32))
    Wi = np.ascontiguousarray(np.asarray(L0_attn_Wi, dtype=np.float32))
    Wo = np.ascontiguousarray(np.asarray(L0_attn_Wo, dtype=np.float32))
    gw = np.asarray(L0_attn_g_weight, dtype=np.float32)
    Wn = np.ascontiguousarray(np.asarray(L0_ffn_weight, dtype=np.float32))
    bias = np.asarray(L0_ffn_bias, dtype=np.float32)

    ident = np.eye(128, dtype=np.float32).astype(BFD)
    tau = np.arange(128)[:, None]
    tcol = np.arange(128)[None, :]
    lcm = ((tau <= tcol).astype(np.float32)
           - ((tau <= 63).astype(np.float32)
              * np.ones((1, 128), np.float32))).astype(BFD)
    u2m = np.concatenate(
        [(np.arange(128) <= 63).astype(np.float32)[:, None],
         np.ones((128, 1), np.float32)], 1).astype(BFD)
    u64 = (np.arange(128) >= 64).astype(np.float32)[:, None].astype(BFD)
    masku = (tau <= tcol).astype(np.uint8)
    ones1 = np.ones((1, 128), np.float32).astype(BFD)
    onesc = np.ones((128, 1), np.float32).astype(BFD)
    sel8 = np.zeros((8, 1024), np.float32)
    for c in range(8):
        sel8[c, c * 128:(c + 1) * 128] = 1.0
    sel8 = sel8.astype(BFD)
    gwcol = np.ascontiguousarray(gw.reshape(16, 128).T)
    biasr = np.ascontiguousarray(bias.reshape(1, 2048))

    in_maps = []
    for r in range(NCORE):
        pm = np.zeros((128, 8), np.float32)
        pidx = np.zeros((1, 8), np.int32)
        for j in range(1, 8):
            p = r - j
            if p >= 0:
                pm[:, j] = 1.0
                pidx[0, j] = p
        in_maps.append(dict(
            xin=np.ascontiguousarray(x[r * T:(r + 1) * T]),
            wf=Wf, wi=Wi, wq=Wq, wo=Wo,
            wns=np.ascontiguousarray(Wn[r * 256:(r + 1) * 256]),
            biasr=biasr, gwcol=gwcol,
            identin=ident, lcin=lcm, u2in=u2m, u64in=u64,
            maskin=masku, ones1in=ones1, onescin=onesc, sel8in=sel8,
            pmaskin=pm, pidxin=pidx,
            pltrin=np.tile((np.arange(8) < r).astype(np.float32)[None, :],
                           (128, 1)),
            cstin=np.tile(np.array([[1.0, LOG_DKSC, EPS_LN, 0.0]],
                                   np.float32), (128, 1)),
        ))
    nc = _get_nc()
    res = run_bass_kernel_spmd(nc, in_maps, core_ids=list(range(NCORE)))
    global _LAST_RES
    _LAST_RES = res.results
    out = np.concatenate([res.results[r]["out"] for r in range(NCORE)], axis=0)
    return np.ascontiguousarray(out.reshape(1, 8192, 2048), dtype=np.float32)


if __name__ == "__main__":
    nc = build_nc()
    n = sum(1 for _ in nc.all_instructions())
    print("built ok, instructions:", n)


# version 09
# speedup vs baseline: 1.0354x; 1.0354x over previous
import contextlib
import os
import numpy as np
import concourse.bass as bass
import concourse.mybir as mybir
from concourse.bass_utils import run_bass_kernel_spmd

F32 = mybir.dt.float32
BF16 = mybir.dt.bfloat16
U8 = mybir.dt.uint8
I32 = mybir.dt.int32
AF = mybir.ActivationFunctionType
ALU = mybir.AluOpType
AX = mybir.AxisListType.X

T = 1024
NCH = 8
NBLK = 4
OB = 512
HCN = 16
NH = 16
NHB = 4
NCORE = 8
LOG_DKSC = float(-0.5 * np.log(128.0))
EPS_LN = 1e-5
EPS_RMS = 1e-5

ENG = ("pe", "act", "dve", "syq", "gpq")
ENG2SEC = {"pe": "tensor", "act": "scalar", "dve": "vector",
           "syq": "sync", "gpq": "gpsimd"}
CENG = ("pe", "act", "dve")
PSUMRES = frozenset("p%d" % i for i in range(8))


class B:

    def __init__(self, nc, es):
        self.nc = nc
        self.es = es
        self.prog = {k: [] for k in ENG}
        self.sems = {k: es.enter_context(nc.semaphore("s_" + k)) for k in ENG}
        self.xsems = {}
        self.cnt = {k: 0 for k in ENG}
        self.hwm = {k: {} for k in ENG}
        self.res = {}

    def xsem(self, name):
        if name not in self.xsems:
            self.xsems[name] = self.es.enter_context(self.nc.semaphore(name))
        return self.xsems[name]

    def _wait(self, eng, semkey, val):
        if semkey in ("syq", "gpq"):
            val = self.cnt[semkey]
        if val <= 0 or self.hwm[eng].get(semkey, 0) >= val:
            return
        self.hwm[eng][semkey] = val
        s = self.sems.get(semkey) or self.xsems[semkey]
        self.prog[eng].append(lambda e, s=s, v=val: e.wait_ge(s, v))

    def op(self, eng, fn, reads=(), writes=(), waits=(), inc=None):
        if inc is None:
            inc = 16 if eng in ("syq", "gpq") else 1
        reads = tuple(reads)
        writes = tuple(writes)
        excl = tuple(r for r in reads if r in PSUMRES)
        if excl:
            reads = tuple(r for r in reads if r not in PSUMRES)
            writes = writes + excl
        need = list(waits)
        for r in reads:
            g = self.res.get(r)
            if g and g["w"] is not None:
                we = g["w"][0]
                if we != eng or eng != "pe":
                    need.append(g["w"])
        for w in writes:
            g = self.res.get(w)
            if g:
                if g["w"] is not None:
                    we = g["w"][0]
                    if we != eng or eng in ("act", "dve", "syq", "gpq"):
                        need.append(g["w"])
                for t in g["r"]:
                    if t[0] != eng or eng in ("syq", "gpq"):
                        need.append(t)
        for semkey, val in need:
            self._wait(eng, semkey, val)
        self.cnt[eng] += inc
        tag = (eng, self.cnt[eng])
        s = self.sems[eng]
        self.prog[eng].append(lambda e, fn=fn, s=s, inc=inc:
                              fn(e).then_inc(s, inc))
        for r in reads:
            self.res.setdefault(r, {"w": None, "r": []})["r"].append(tag)
        for w in writes:
            self.res[w] = {"w": tag, "r": []}
        return tag

    def mark_write(self, resname, semkey, val):
        self.res[resname] = {"w": (semkey, val), "r": []}

    def barrier(self):
        for g in self.res.values():
            if g["w"] is not None and g["w"][0] in CENG:
                g["w"] = None
            g["r"] = [t for t in g["r"] if t[0] not in CENG]

    def run_block(self):
        nc = self.nc
        prog = {k: list(v) for k, v in self.prog.items()}
        for k in ENG:
            self.prog[k] = []
        with nc.Block() as blk:
            for k in ENG:
                if not prog[k]:
                    continue
                dec = getattr(blk, ENG2SEC[k])
                def make(ops):
                    def body(eng):
                        for f in ops:
                            f(eng)
                    return body
                dec(make(prog[k]))
        self.barrier()


def pipeline(stages, n):
    maxlag = max(l for l, _ in stages)
    for it in range(n + maxlag):
        for lag, fn in stages:
            i = it - lag
            if 0 <= i < n:
                fn(i)


_MAXPH = int(os.environ.get("KMAXPH", "99"))
_P2N = int(os.environ.get("KP2N", "32"))
_NOCOLL = int(os.environ.get("KNOCOLL", "0"))
_STAGES = os.environ.get("KSTAGES", "ACGTBEKD")
_DBG = int(os.environ.get("KDBG", "0"))


def build_nc():
    nc = bass.Bass("TRN2", target_bir_lowering=False)

    xin = nc.dram_tensor("xin", [T, 2048], F32, kind="ExternalInput").ap()
    wdr = [nc.dram_tensor(nm, [2048, 2048], F32, kind="ExternalInput").ap()
           for nm in ("wf", "wi", "wq", "wo")]
    wns = nc.dram_tensor("wns", [256, 2048], F32, kind="ExternalInput").ap()
    biasd = nc.dram_tensor("biasr", [1, 2048], F32, kind="ExternalInput").ap()
    gwd = nc.dram_tensor("gwcol", [128, 16], F32, kind="ExternalInput").ap()
    identd = nc.dram_tensor("identin", [128, 128], BF16, kind="ExternalInput").ap()
    lcd = nc.dram_tensor("lcin", [128, 128], BF16, kind="ExternalInput").ap()
    u2d = nc.dram_tensor("u2in", [128, 2], BF16, kind="ExternalInput").ap()
    u64d = nc.dram_tensor("u64in", [128, 1], BF16, kind="ExternalInput").ap()
    maskd = nc.dram_tensor("maskin", [128, 128], U8, kind="ExternalInput").ap()
    ones1d = nc.dram_tensor("ones1in", [1, 128], BF16, kind="ExternalInput").ap()
    onescd = nc.dram_tensor("onescin", [128, 1], BF16, kind="ExternalInput").ap()
    sel8d = nc.dram_tensor("sel8in", [8, 1024], BF16, kind="ExternalInput").ap()
    pmaskd = nc.dram_tensor("pmaskin", [128, 8], F32, kind="ExternalInput").ap()
    pidxd = nc.dram_tensor("pidxin", [1, 8], I32, kind="ExternalInput").ap()
    pltrd = nc.dram_tensor("pltrin", [128, 8], F32, kind="ExternalInput").ap()
    cstd = nc.dram_tensor("cstin", [128, 4], F32, kind="ExternalInput").ap()
    outd = nc.dram_tensor("out", [T, 2048], F32, kind="ExternalOutput").ap()
    dbgd = {}
    if _DBG:
        for nm, shape, dt in (("dln", [128, 16384], BF16),
                              ("dkb", [128, 16384], BF16),
                              ("dosb", [128, 16384], BF16),
                              ("dhsb", [128, 2048], F32),
                              ("deD", [T, 2048], BF16),
                              ("dvD", [T, 2048], BF16),
                              ("drows", [8, 2048], BF16),
                              ("decols", [128, 256], F32),
                              ("dgcol", [128, 16], F32),
                              ("dzbf", [128, 128], BF16),
                              ("dA", [128, 128], BF16),
                              ("dqT", [128, 128], BF16),
                              ("dkT", [128, 128], BF16),
                              ("dx1", [128, 16384], F32)):
            dbgd[nm] = nc.dram_tensor(nm, shape, dt, kind="ExternalOutput").ap()

    eD = nc.dram_tensor("eD", [T, 2048], BF16).ap()
    vD = nc.dram_tensor("vD", [T, 2048], BF16).ap()
    gin = nc.dram_tensor("gin", [2048, 128], F32).ap()
    gin2 = nc.dram_tensor("gin2", [128, 16], F32).ap()
    goutb_h = [nc.dram_tensor("goutb%d" % i, [NCORE * 512, 128], F32)
               for i in range(4)]
    gout2_h = nc.dram_tensor("gout2", [NCORE * 128, 16], F32)
    ginq = nc.dram_tensor("ginq", [2048, 256], BF16).ap()
    goutq = nc.dram_tensor("goutq", [NCORE * 2048, 256], BF16).ap()
    goutb = [h.ap() for h in goutb_h]
    gout2 = gout2_h.ap()

    es = contextlib.ExitStack()
    with es:
        def sbt(name, shape, dt):
            return es.enter_context(nc.sbuf_tensor(name, shape, dt))

        ident = sbt("identb", [128, 128], BF16)
        lc = sbt("lcb", [128, 128], BF16)
        u2 = sbt("u2b", [128, 2], BF16)
        u64c = sbt("u64b", [128, 1], BF16)
        masku = sbt("maskb", [128, 128], U8)
        ones1 = sbt("ones1b", [1, 128], BF16)
        onesc = sbt("onescb", [128, 1], BF16)
        sel8 = sbt("sel8b", [8, 1024], BF16)
        gwcol = sbt("gwb", [128, 16], F32)
        pmask = sbt("pmb", [128, 8], F32)
        pidx = sbt("pib", [1, 8], I32)
        pltr = sbt("pltrb", [128, 8], F32)
        zbf = sbt("zbf", [128, 128], BF16)
        cstb = sbt("cstb", [128, 4], F32)
        nc.const_aps.register(F32, 1.0, cstb[:, 0:1])
        nc.const_aps.register(F32, LOG_DKSC, cstb[:, 1:2])
        nc.const_aps.register(F32, EPS_LN, cstb[:, 2:3])
        ln = sbt("ln", [128, HCN * 1024], BF16)
        osb = sbt("osb", [128, NH * 1024], BF16)
        wst = sbt("wst", [128, 2 * HCN * OB], BF16)
        rows8 = sbt("rows8", [8, 2048], BF16)
        ecols = sbt("ecols", [128, 256], F32)
        gcol = sbt("gcolb", [128, 16], F32)
        hsb = sbt("hsb", [128, NH * 128], F32)
        st1 = sbt("st1", [128, 2], F32)
        st2 = sbt("st2", [128, 2], F32)
        smean = sbt("smean", [128, 2], F32)
        svar = sbt("svar", [128, 2], F32)
        srstd = sbt("srstd", [128, 2], F32)
        sm2 = sbt("sm2", [128, 2], F32)

        es1 = contextlib.ExitStack()
        def sb1(name, shape, dt):
            return es1.enter_context(nc.sbuf_tensor(name, shape, dt, side="right"))
        wq32 = sb1("wq32", [128, 2 * 2048], F32)
        awb = sb1("awb", [128, 2048], F32)
        sgnb = sb1("sgnb", [128, 2048], BF16)
        mskq = sb1("mskqb", [128, 2048], BF16)
        qtb = sb1("qtb", [128, 2048], BF16)
        qstage = sb1("qstage", [128, HCN * 128], BF16)
        thr = sb1("thrb", [128, 1], F32)
        xts = sb1("xts", [128, 2 * 2048], F32)
        xsq = sb1("xsq", [128, 2048], F32)
        ltst = sb1("ltst", [128, 2 * 2048], BF16)
        pbank = [es.enter_context(nc.psum_tensor("pb%d" % i, [128, 512], F32))
                 for i in range(8)]

        rgH = es.enter_context(nc.gpsimd.register("rgH"))
        rgG = es.enter_context(nc.gpsimd.register("rgG"))
        b = B(nc, es)
        dump_src = {}

        def do_dump():
            if not _DBG:
                return
            for nm, (srcap, res) in list(dump_src.items()):
                dma("syq", dbgd[nm], srcap, reads=(res,))
            vv = b.cnt["syq"]
            b.prog["syq"].append(
                lambda e, s=b.sems["syq"], v=vv: e.wait_ge(s, v))
            b.run_block()
        for s in ("cc_q", "cc_h", "cc_g"):
            b.xsem(s)

        def dma(q, dst, src, reads=(), writes=()):
            return b.op(q, lambda e, d=dst, s=src: e.dma_start(d, s),
                        reads=reads, writes=writes)

        def mm(out_ap, lhsT, rhs, start, stop, reads=(), writes=()):
            return b.op("pe", lambda e, o=out_ap, l=lhsT, r=rhs, st=start,
                        sp=stop: e.matmul(o, l, r, start=st, stop=sp),
                        reads=reads, writes=writes)

        def wsl(idx, hc):
            s0 = (idx % 2) * HCN * OB
            return wst[:, s0 + hc * OB: s0 + (hc + 1) * OB]

        def issue_wload(idx):
            blk = idx % 4
            slot = "wslot%d" % (idx % 2)
            s0 = (idx % 2) * HCN * OB
            if idx < 16:
                src = wdr[idx // 4][:, blk * OB:(blk + 1) * OB].rearrange(
                    "(hc p) o -> p hc o", p=128)
                dst = wst[:, s0:s0 + HCN * OB].rearrange(
                    "p (hc o) -> p hc o", hc=HCN)
                dma("gpq", dst, src, writes=(slot,))
            else:
                full = wst[:, s0:s0 + HCN * OB].rearrange(
                    "p (ic o) -> p ic o", o=OB)
                for half in range(2):
                    rank = blk * 2 + half
                    src = goutq[rank * 2048:(rank + 1) * 2048, :].rearrange(
                        "(ic p) o -> p ic o", p=128)
                    dma("gpq", full[:, :, half * 256:(half + 1) * 256], src,
                        reads=("goutq",), writes=(slot,))

        _wload_next = [0]

        def ensure_load(upto):
            while _wload_next[0] <= min(upto, 19):
                issue_wload(_wload_next[0])
                _wload_next[0] += 1

        def proj_mms(idx, c, pk, pres):
            for hc in range(HCN):
                lnsl = ln[:, hc * 1024 + c * 128: hc * 1024 + (c + 1) * 128]
                mm(pk[:, :], lnsl, wsl(idx, hc), hc == 0, hc == HCN - 1,
                   reads=("wslot%d" % (idx % 2), "lnr"), writes=(pres,))

        def emit_coll(kind, ins, outs, semname, waits=()):
            s = b.xsem(semname)
            if _NOCOLL:
                b.prog["gpq"].append(
                    lambda e, s=s: e.engine_nop().then_inc(s, 1))
                return
            for semkey, val in waits:
                b._wait("gpq", semkey, val)
            def f(e, s=s, ins=ins, outs=outs, kind=kind):
                e.collective_compute(
                    kind, ALU.bypass, replica_groups=[list(range(NCORE))],
                    ins=ins, outs=outs).then_inc(s, 1)
            b.prog["gpq"].append(f)

        for dst, src in ((ident, identd), (lc, lcd), (u2, u2d), (u64c, u64d),
                         (masku, maskd), (ones1, ones1d), (onesc, onescd),
                         (sel8, sel8d), (gwcol, gwd), (pmask, pmaskd),
                         (pidx, pidxd), (pltr, pltrd), (cstb, cstd)):
            dma("syq", dst[:, :], src, writes=("consts",))
        for rc in range(2):
            dma("syq", wq32[:, rc * 2048:(rc + 1) * 2048],
                wns[rc * 128:(rc + 1) * 128, :], writes=("wq32_%d" % rc,))
        ensure_load(1)
        b.op("dve", lambda e: e.memset(zbf[:, :], 0.0), writes=("zbf",))
        b.op("dve", lambda e: e.memset(gcol[:, :], 0.0), writes=("gcol",))
        b.op("dve", lambda e: e.memset(hsb[:, :], 0.0),
             writes=("H0", "H1", "H2", "H3"))
        b.run_block()
        if _MAXPH <= 1:
            do_dump()
            return nc

        for rc in range(2):
            wq_ = wq32[:, rc * 2048:(rc + 1) * 2048]
            rrc = ("wq32_%d" % rc,)
            b.op("dve", lambda e, w=wq_: e.reduce_sum(
                thr[:, :], w, axis=AX, apply_absolute_value=True),
                reads=rrc, writes=("thr",))
            b.op("dve", lambda e: e.tensor_scalar_mul(
                thr[:, :], thr[:, :], 0.7 / 2048.0),
                reads=("thr",), writes=("thr",))
            b.op("act", lambda e, w=wq_: e.activation(awb[:, :], w, AF.Abs),
                 reads=rrc, writes=("awq",))
            b.op("act", lambda e, w=wq_: e.activation(sgnb[:, :], w, AF.Sign),
                 reads=rrc, writes=("sgnq",))
            b.op("dve", lambda e: e.tensor_single_scalar(
                mskq[:, :], awb[:, :], thr[:, :], ALU.is_gt),
                reads=("awq", "thr"), writes=("mskq",))
            b.op("dve", lambda e: e.tensor_tensor(
                qtb[:, :], sgnb[:, :], mskq[:, :], ALU.mult),
                reads=("sgnq", "mskq"), writes=("qtb",))
            for ic in range(HCN):
                pk = pbank[6 + (ic % 2)]
                pr = "p%d" % (6 + ic % 2)
                mm(pk[:, 0:128], qtb[:, ic * 128:(ic + 1) * 128], ident[:, :],
                   True, True, reads=("qtb", "consts"), writes=(pr,))
                b.op("dve", lambda e, pk=pk, ic=ic: e.tensor_copy(
                    qstage[:, ic * 128:(ic + 1) * 128], pk[:, 0:128]),
                    reads=(pr,), writes=("qstage",))
            dma("syq",
                ginq[:, rc * 128:(rc + 1) * 128].rearrange(
                    "(ic p) o -> p ic o", p=128),
                qstage.rearrange("p (ic o) -> p ic o", o=128),
                reads=("qstage",), writes=("ginq",))

        def emit_ln(t, src_dram, tag):
            par = t % 2
            xq = xsq[:, 0:2048]
            lt = ltst[:, par * 2048:(par + 1) * 2048]
            c1 = st1[:, par:par + 1]
            c2 = st2[:, par:par + 1]
            cm = smean[:, par:par + 1]
            cv = svar[:, par:par + 1]
            cr = srstd[:, par:par + 1]
            c5 = sm2[:, par:par + 1]
            R1, R2, RM, RV, RR, R5 = ("st1%d" % par, "st2%d" % par,
                                      "smean%d" % par, "svar%d" % par,
                                      "srstd%d" % par, "sm2%d" % par)
            if src_dram is not None:
                xt = xts[:, par * 2048:(par + 1) * 2048]
                xr = ("xts%d" % par,)
                dma("syq", xt, src_dram[t * 128:(t + 1) * 128, :], writes=xr)
            else:
                xt = x1[:, t * 2048:(t + 1) * 2048]
                xr = ("x1_%d" % t,)
            b.op("dve", lambda e, xt=xt, c1=c1: e.reduce_sum(c1, xt, axis=AX),
                 reads=xr, writes=(R1,))
            b.op("act", lambda e, xt=xt, xq=xq, c2=c2: e.activation(
                xq, xt, AF.Square, accum_out=c2),
                reads=xr, writes=("xsq", R2))
            b.op("dve", lambda e, c1=c1, cm=cm: e.tensor_scalar_mul(
                cm, c1, 1.0 / 2048.0), reads=(R1,), writes=(RM,))
            b.op("dve", lambda e, c2=c2, cv=cv: e.tensor_scalar_mul(
                cv, c2, 1.0 / 2048.0), reads=(R2,), writes=(RV,))
            b.op("dve", lambda e, cm=cm, c5=c5: e.tensor_tensor(
                c5, cm, cm, ALU.mult), reads=(RM,), writes=(R5,))
            b.op("dve", lambda e, cv=cv, c5=c5: e.tensor_tensor(
                cv, cv, c5, ALU.subtract), reads=(RV, R5), writes=(RV,))
            b.op("act", lambda e, cv=cv: e.activation(
                cv, cv, AF.Ln, bias=EPS_LN), reads=(RV,), writes=(RV,))
            b.op("act", lambda e, cv=cv, cr=cr: e.activation(
                cr, cv, AF.Exp, scale=-0.5), reads=(RV,), writes=(RR,))
            b.op("dve", lambda e, xt=xt, lt=lt, cm=cm, cr=cr: e.tensor_scalar(
                lt, xt, cm, cr, ALU.subtract, ALU.mult),
                reads=xr + (RM, RR), writes=("ltst%d" % par,))
            for hc in range(HCN):
                pk = pbank[6 + (hc % 2)]
                pr = "p%d" % (6 + hc % 2)
                mm(pk[:, 0:128], lt[:, hc * 128:(hc + 1) * 128], ident[:, :],
                   True, True, reads=("ltst%d" % par, "consts"), writes=(pr,))
                b.op("dve", lambda e, pk=pk, hc=hc, t=t: e.tensor_copy(
                    ln[:, hc * 1024 + t * 128: hc * 1024 + (t + 1) * 128],
                    pk[:, 0:128]), reads=(pr,), writes=("lnr",))

        for t in range(NCH):
            emit_ln(t, xin, "lnr")
        if _DBG:
            dump_src["dln"] = (ln[:, :], "lnr")
        b.run_block()
        if _MAXPH <= 2:
            do_dump()
            return nc

        es1.close()
        es2 = contextlib.ExitStack()
        def sb2(name, shape, dt):
            return es2.enter_context(nc.sbuf_tensor(name, shape, dt, side="right"))
        kb = sb2("kbt", [128, NCH * 2048], BF16)
        es2a = contextlib.ExitStack()
        def sb2a(name, shape, dt):
            return es2a.enter_context(nc.sbuf_tensor(name, shape, dt, side="right"))
        ta = sb2a("ta", [128, 3 * OB], F32)
        te = sb2a("te", [128, 3 * OB], F32)
        tf = sb2a("tf", [128, 3 * OB], BF16)
        fmn = sb2a("fmn", [128, 3 * OB], F32)
        g32 = sb2a("g32", [128, 3 * OB], F32)
        ghi = sb2a("ghi", [128, 3 * OB], BF16)
        glo = sb2a("glo", [128, 3 * OB], BF16)
        est = sb2a("est", [128, 3 * OB], BF16)
        srst = sb2a("srst", [1, 3 * OB], BF16)

        emit_coll("AllGather", [ginq.opt()], [goutq.opt()], "cc_q",
                  waits=[("syq", b.cnt["syq"])])
        b.mark_write("goutq", "cc_q", 1)

        FB = (0, 1, 6)
        GB = (2, 3, 7)

        def p2_A(i):
            blk, c = divmod(i, NCH)
            if c == 0:
                ensure_load(blk)
            proj_mms(blk, c, pbank[FB[i % 3]], "p%d" % FB[i % 3])
            if c == NCH - 1:
                ensure_load(blk + 1)

        def p2_C(i):
            par = i % 3
            fres = "p%d" % FB[par]
            taa = ta[:, par * OB:(par + 1) * OB]
            fp = pbank[FB[par]]
            b.op("act", lambda e, fp=fp, taa=taa: e.activation(
                taa, fp[:, :], AF.Abs), reads=(fres,), writes=("ta%d" % par,))
            b.op("act", lambda e, taa=taa: e.activation(
                taa, taa, AF.Exp, scale=-1.0),
                reads=("ta%d" % par,), writes=("ta%d" % par,))
            b.op("act", lambda e, taa=taa: e.activation(
                taa, taa, AF.Ln, bias=1.0),
                reads=("ta%d" % par,), writes=("ta%d" % par,))

        def p2_G(i):
            par = i % 3
            fres = "p%d" % FB[par]
            fp = pbank[FB[par]]
            fmm = fmn[:, par * OB:(par + 1) * OB]
            gg = g32[:, par * OB:(par + 1) * OB]
            gh = ghi[:, par * OB:(par + 1) * OB]
            gl = glo[:, par * OB:(par + 1) * OB]
            taa = ta[:, par * OB:(par + 1) * OB]
            b.op("dve", lambda e, fp=fp, fmm=fmm: e.tensor_scalar_min(
                fmm, fp[:, :], 0.0), reads=(fres,), writes=("fmn%d" % par,))
            b.op("dve", lambda e, fmm=fmm, gg=gg, taa=taa: e.tensor_tensor(
                gg, fmm, taa, ALU.subtract),
                reads=("fmn%d" % par, "ta%d" % par), writes=("g%d" % par,))
            b.op("dve", lambda e, gg=gg, gh=gh: e.tensor_copy(gh, gg),
                 reads=("g%d" % par,), writes=("gh%d" % par,))
            b.op("dve", lambda e, gg=gg, gh=gh, gl=gl: e.tensor_tensor(
                gl, gg, gh, ALU.subtract),
                reads=("g%d" % par, "gh%d" % par), writes=("gl%d" % par,))

        def p2_T(i):
            par = i % 3
            tee = te[:, par * OB:(par + 1) * OB]
            gg = g32[:, par * OB:(par + 1) * OB]
            b.op("act", lambda e, gg=gg, tee=tee: e.activation(
                tee, gg, AF.Exp), reads=("g%d" % par,), writes=("te%d" % par,))

        def p2_B(i):
            par = i % 3
            gh = ghi[:, par * OB:(par + 1) * OB]
            gl = glo[:, par * OB:(par + 1) * OB]
            gp = pbank[GB[par]]
            gres = "p%d" % GB[par]
            mm(gp[:, :], lc[:, :], gh, True, False,
               reads=("gh%d" % par, "consts"), writes=(gres,))
            mm(gp[:, :], lc[:, :], gl, False, True,
               reads=("gl%d" % par,), writes=(gres,))
            rp = pbank[4]
            mm(rp[0:1, :], u64c[:, :], gh, True, False,
               reads=("gh%d" % par,), writes=("p4",))
            mm(rp[0:1, :], u64c[:, :], gl, False, True,
               reads=("gl%d" % par,), writes=("p4",))
            cp = pbank[5]
            for hl in range(NHB):
                mm(cp[:, hl * 2: hl * 2 + 2],
                   gh[:, hl * 128:(hl + 1) * 128], u2[:, :], True, False,
                   reads=("gh%d" % par,), writes=("p5",))
                mm(cp[:, hl * 2: hl * 2 + 2],
                   gl[:, hl * 128:(hl + 1) * 128], u2[:, :], False, True,
                   reads=("gl%d" % par,), writes=("p5",))

        def p2_E(i):
            blk, c = divmod(i, NCH)
            par = i % 3
            gp = pbank[GB[par]]
            gres = "p%d" % GB[par]
            ee = est[:, par * OB:(par + 1) * OB]
            tff = tf[:, par * OB:(par + 1) * OB]
            sr = srst[0:1, par * OB:(par + 1) * OB]
            ecsl = ecols[:, (c * 16 + blk * 4) * 2: (c * 16 + blk * 4) * 2 + 8]
            b.op("act", lambda e, gp=gp, ee=ee: e.activation(
                ee, gp[:, :], AF.Exp, bias=LOG_DKSC),
                reads=(gres,), writes=("est%d" % par,))
            b.op("act", lambda e, gp=gp, tff=tff: e.activation(
                tff, gp[:, :], AF.Exp, scale=-1.0),
                reads=(gres,), writes=("tf%d" % par,))
            b.op("act", lambda e, sr=sr: e.activation(
                sr, pbank[4][0:1, :], AF.Exp),
                reads=("p4",), writes=("sr%d" % par,))
            b.op("act", lambda e, ecsl=ecsl: e.activation(
                ecsl, pbank[5][:, 0:8], AF.Exp),
                reads=("p5",), writes=("ecols",))

        def p2_K(i):
            blk, c = divmod(i, NCH)
            par = i % 3
            tee = te[:, par * OB:(par + 1) * OB]
            tff = tf[:, par * OB:(par + 1) * OB]
            kbsl = kb[:, c * 2048 + blk * OB: c * 2048 + (blk + 1) * OB]
            b.op("dve", lambda e, tee=tee: e.tensor_scalar(
                tee, tee, -1.0, -1.0, ALU.add, ALU.mult),
                reads=("te%d" % par,), writes=("te%d" % par,))
            b.op("dve", lambda e, tee=tee, tff=tff, kbsl=kbsl: e.tensor_tensor(
                kbsl, tee, tff, ALU.mult),
                reads=("te%d" % par, "tf%d" % par), writes=("kbr",))
            gcsl = gcol[:, blk * 4:(blk + 1) * 4]
            cp = pbank[5]
            b.op("dve", lambda e, gcsl=gcsl, cp=cp: e.tensor_tensor(
                gcsl, gcsl, cp[:, 1:8:2], ALU.add),
                reads=("gcol", "p5"), writes=("gcol",))

        def p2_D(i):
            blk, c = divmod(i, NCH)
            par = i % 3
            ee = est[:, par * OB:(par + 1) * OB]
            sr = srst[0:1, par * OB:(par + 1) * OB]
            dma("syq", eD[c * 128:(c + 1) * 128, blk * OB:(blk + 1) * OB],
                ee, reads=("est%d" % par,), writes=("eDram",))
            dma("gpq", rows8[c:c + 1, blk * OB:(blk + 1) * OB], sr,
                reads=("sr%d" % par,), writes=("rows8",))

        _all = [("A", 0, p2_A), ("C", 1, p2_C), ("G", 1, p2_G),
                ("T", 2, p2_T), ("E", 3, p2_E), ("K", 3, p2_K),
                ("D", 3, p2_D), ("B", 2, p2_B)]
        pipeline([(l, f) for nm, l, f in _all if nm in _STAGES], _P2N)
        ensure_load(5)
        if _DBG:
            dump_src["dkb"] = (kb[:, :], "kbr")
            dump_src["deD"] = (eD, "eDram")
            dump_src["drows"] = (rows8[:, :], "rows8")
            dump_src["decols"] = (ecols[:, :], "ecols")
            dump_src["dgcol"] = (gcol[:, :], "gcol")
        b.run_block()
        if _MAXPH <= 3:
            do_dump()
            return nc

        es2a.close()
        es2b = contextlib.ExitStack()
        def sb2b(name, shape, dt):
            return es2b.enter_context(nc.sbuf_tensor(name, shape, dt, side="right"))
        vblk = sb2b("vblk", [128, 2 * NCH * OB], BF16)
        eblk = sb2b("eblk", [128, 2 * NCH * OB], BF16)
        kdst = sb2b("kdst", [128, 2 * OB], BF16)
        qst = sb2b("qstb", [128, 2 * OB], BF16)
        qbst = sb2b("qbst", [128, 2 * OB], BF16)
        qbT = sb2b("qbTb", [128, 2 * 128], BF16)
        kbT = sb2b("kbTb", [128, 2 * 128], BF16)
        asb = sb2b("asbb", [128, 2 * 128], BF16)
        hpsb = sb2b("hpsb", [128, 2 * 128], BF16)
        hp2 = sb2b("hp2", [128, 2 * 512], F32)
        g2all = sb2b("g2all", [128, 128], F32)
        sall = sb2b("sall", [128, 128], F32)
        smy = sb2b("smy", [128, 16], F32)
        sclall = sb2b("sclall", [128, 128], F32)
        tdif = sb2b("tdif", [128, 16], F32)

        dma("syq", gin2[:, :], gcol[:, :], reads=("gcol",), writes=("gin2",))
        emit_coll("AllGather", [gin2.opt()], [gout2.opt()], "cc_g",
                  waits=[("syq", b.cnt["syq"])])
        b.mark_write("gout2r", "cc_g", 1)

        def p3_A(i):
            blk, c = divmod(i, NCH)
            if c == 0:
                ensure_load(4 + blk)
            proj_mms(4 + blk, c, pbank[i % 2], "p%d" % (i % 2))
            if c == NCH - 1:
                ensure_load(4 + blk + 1)

        def p3_V(i):
            blk, c = divmod(i, NCH)
            par = i % 2
            ip = pbank[par]
            bp = blk % 2
            vsl = vblk[:, bp * NCH * OB + c * OB: bp * NCH * OB + (c + 1) * OB]
            b.op("dve", lambda e, ip=ip, vsl=vsl: e.tensor_copy(vsl, ip[:, :]),
                 reads=("p%d" % par,), writes=("vb%d_%d" % (bp, c),))

        def p3_S(i):
            blk, c = divmod(i, NCH)
            sp = pbank[2]
            mm(sp[:, :], sel8[:, c * 128:(c + 1) * 128],
               rows8[0:8, blk * OB:(blk + 1) * OB], True, True,
               reads=("rows8", "consts"), writes=("p2",))

        def p3_K(i):
            blk, c = divmod(i, NCH)
            par = i % 2
            kd = kdst[:, par * OB:(par + 1) * OB]
            kbsl = kb[:, c * 2048 + blk * OB: c * 2048 + (blk + 1) * OB]
            sp = pbank[2]
            b.op("dve", lambda e, kd=kd, kbsl=kbsl, sp=sp: e.tensor_tensor(
                kd, kbsl, sp[:, :], ALU.mult),
                reads=("kbr", "p2"), writes=("kd%d" % par,))

        def p3_W(i):
            blk, c = divmod(i, NCH)
            par = i % 2
            bp = blk % 2
            kd = kdst[:, par * OB:(par + 1) * OB]
            vsl = vblk[:, bp * NCH * OB + c * OB: bp * NCH * OB + (c + 1) * OB]
            for hl in range(NHB):
                hg = blk * 4 + hl
                hup = pbank[3]
                mm(hup[:, 0:128], kd[:, hl * 128:(hl + 1) * 128],
                   vsl[:, hl * 128:(hl + 1) * 128], True, True,
                   reads=("kd%d" % par, "vb%d_%d" % (bp, c)), writes=("p3",))
                hsl = hsb[:, hg * 128:(hg + 1) * 128]
                ecsl = ecols[:, (c * 16 + hg) * 2 + 1: (c * 16 + hg) * 2 + 2]
                b.op("dve", lambda e, hsl=hsl, ecsl=ecsl, hup=hup:
                     e.scalar_tensor_tensor(hsl, hsl, ecsl, hup[:, 0:128],
                                            ALU.mult, ALU.add),
                     reads=("H%d" % blk, "ecols", "p3"), writes=("H%d" % blk,))
            if c == NCH - 1:
                vfull = vblk[:, bp * NCH * OB:(bp + 1) * NCH * OB].rearrange(
                    "p (cc d) -> p cc d", d=OB)
                dma("syq", vD[:, blk * OB:(blk + 1) * OB].rearrange(
                    "(cc p) d -> p cc d", p=128), vfull,
                    reads=tuple("vb%d_%d" % (bp, cc) for cc in range(NCH)),
                    writes=("vDram%d" % blk,))
                dma("syq",
                    gin[blk * 512:(blk + 1) * 512, :].rearrange(
                        "(h d) v -> d h v", d=128),
                    hsb[:, blk * 512:(blk + 1) * 512].rearrange(
                        "p (h v) -> p h v", v=128),
                    reads=("H%d" % blk,), writes=("ginb%d" % blk,))
                emit_coll("AllGather", [gin[blk * 512:(blk + 1) * 512, :].opt()],
                          [goutb[blk].opt()], "cc_h%d" % blk,
                          waits=[("syq", b.cnt["syq"])])
                b.mark_write("goutr%d" % blk, "cc_h%d" % blk, 1)

        pipeline([(0, p3_A), (1, p3_V), (2, p3_K), (1, p3_S), (2, p3_W)],
                 NBLK * NCH)
        ensure_load(9)
        if _DBG:
            dump_src["dvD"] = (vD, "vDram3")
            dump_src["dhsb"] = (hsb[:, :], "H3")
        b.run_block()
        if _MAXPH <= 4:
            do_dump()
            return nc

        for blk in range(4):
            b.op("dve", lambda e, blk=blk: e.memset(
                hsb[:, blk * 512:(blk + 1) * 512], 0.0),
                reads=("ginb%d" % blk,), writes=("H%d" % blk,))
        dma("syq", g2all.rearrange("p (c h) -> p c h", h=16),
            gout2.rearrange("(c d) h -> d c h", d=128),
            reads=("gout2r",), writes=("g2all",))
        b.op("dve", lambda e: e.tensor_copy(sall[:, 0:16], g2all[:, 0:16]),
             reads=("g2all",), writes=("sall",))
        for p in range(1, 8):
            b.op("dve", lambda e, p=p: e.tensor_tensor(
                sall[:, p * 16:(p + 1) * 16], sall[:, (p - 1) * 16:p * 16],
                g2all[:, p * 16:(p + 1) * 16], ALU.add),
                reads=("sall", "g2all"), writes=("sall",))
        b.op("dve", lambda e: e.memset(smy[:, :], 0.0), writes=("smy",))
        for p in range(8):
            b.op("dve", lambda e, p=p: e.scalar_tensor_tensor(
                smy[:, :], g2all[:, p * 16:(p + 1) * 16], pltr[:, p:p + 1],
                smy[:, :], ALU.mult, ALU.add),
                reads=("g2all", "consts", "smy"), writes=("smy",))
        for p in range(8):
            b.op("dve", lambda e, p=p: e.tensor_tensor(
                tdif[:, :], smy[:, :], sall[:, p * 16:(p + 1) * 16],
                ALU.subtract), reads=("smy", "sall"), writes=("tdif",))
            b.op("dve", lambda e, p=p: e.tensor_single_scalar(
                tdif[:, :], tdif[:, :], pltr[:, p:p + 1], ALU.mult),
                reads=("tdif", "consts"), writes=("tdif",))
            b.op("act", lambda e, p=p: e.activation(
                sclall[:, p * 16:(p + 1) * 16], tdif[:, :], AF.Exp),
                reads=("tdif",), writes=("sclall",))
            b.op("dve", lambda e, p=p: e.tensor_single_scalar(
                sclall[:, p * 16:(p + 1) * 16],
                sclall[:, p * 16:(p + 1) * 16], pltr[:, p:p + 1], ALU.mult),
                reads=("sclall", "consts"), writes=("sclall",))

        def emit_hin(blk):
            for p in range(8):
                pp = p % 2
                dma("syq", hp2[:, pp * 512:(pp + 1) * 512].rearrange(
                    "p (h v) -> p h v", v=128),
                    goutb[blk][p * 512:(p + 1) * 512, :].rearrange(
                        "(h d) v -> d h v", d=128),
                    reads=("goutr%d" % blk,), writes=("hpb%d" % pp,))
                for hl in range(4):
                    hg = blk * 4 + hl
                    hsl = hsb[:, hg * 128:(hg + 1) * 128]
                    b.op("dve", lambda e, hsl=hsl, hl=hl, pp=pp, p=p, hg=hg:
                         e.scalar_tensor_tensor(
                             hsl,
                             hp2[:, pp * 512 + hl * 128:
                                 pp * 512 + (hl + 1) * 128],
                             sclall[:, p * 16 + hg: p * 16 + hg + 1],
                             hsl, ALU.mult, ALU.add),
                         reads=("hpb%d" % pp, "sclall", "H%d" % blk),
                         writes=("H%d" % blk,))

        def p4_load(blk):
            bp = blk % 2
            dma("syq", eblk[:, bp * NCH * OB:(bp + 1) * NCH * OB].rearrange(
                "p (cc d) -> p cc d", d=OB),
                eD[:, blk * OB:(blk + 1) * OB].rearrange(
                    "(cc p) d -> p cc d", p=128),
                reads=("eDram",), writes=("eblk%d" % bp,))
            dma("syq", vblk[:, bp * NCH * OB:(bp + 1) * NCH * OB].rearrange(
                "p (cc d) -> p cc d", d=OB),
                vD[:, blk * OB:(blk + 1) * OB].rearrange(
                    "(cc p) d -> p cc d", p=128),
                reads=("vDram%d" % blk,), writes=("vbl%d" % bp,))

        p4_load(0)
        p4_load(1)

        def p4_A(i):
            blk, c = divmod(i, NCH)
            if c == 0:
                ensure_load(8 + blk)
                emit_hin(blk)
            proj_mms(8 + blk, c, pbank[i % 2], "p%d" % (i % 2))
            if c == NCH - 1:
                ensure_load(8 + blk + 1)

        def p4_S(i):
            par = i % 2
            qp = pbank[par]
            qs = qst[:, par * OB:(par + 1) * OB]
            b.op("act", lambda e, qp=qp, qs=qs: e.activation(
                qs, qp[:, :], AF.Silu),
                reads=("p%d" % par,), writes=("qs%d" % par,))

        def p4_Q(i):
            blk, c = divmod(i, NCH)
            par = i % 2
            bp = blk % 2
            qs = qst[:, par * OB:(par + 1) * OB]
            qbs = qbst[:, par * OB:(par + 1) * OB]
            eb = eblk[:, bp * NCH * OB + c * OB: bp * NCH * OB + (c + 1) * OB]
            b.op("dve", lambda e, qbs=qbs, qs=qs, eb=eb: e.tensor_tensor(
                qbs, qs, eb, ALU.mult),
                reads=("qs%d" % par, "eblk%d" % bp), writes=("qb%d" % par,))

        def p4_SB(i):
            blk, c = divmod(i, NCH)
            sp = pbank[6]
            mm(sp[:, :], sel8[:, c * 128:(c + 1) * 128],
               rows8[0:8, blk * OB:(blk + 1) * OB], True, True,
               reads=("rows8", "consts"), writes=("p6",))

        def p4_K(i):
            blk, c = divmod(i, NCH)
            par = i % 2
            kd = kdst[:, par * OB:(par + 1) * OB]
            kbsl = kb[:, c * 2048 + blk * OB: c * 2048 + (blk + 1) * OB]
            sp = pbank[6]
            b.op("dve", lambda e, kd=kd, kbsl=kbsl, sp=sp: e.tensor_tensor(
                kd, kbsl, sp[:, :], ALU.mult),
                reads=("kbr", "p6"), writes=("kd%d" % par,))

        def p4_W(i):
            blk, c = divmod(i, NCH)
            par = i % 2
            bp = blk % 2
            qbs = qbst[:, par * OB:(par + 1) * OB]
            kbsl = kb[:, c * 2048 + blk * OB: c * 2048 + (blk + 1) * OB]
            kd = kdst[:, par * OB:(par + 1) * OB]
            for hl in range(NHB):
                hg = blk * 4 + hl
                hpar = hl % 2
                vsl = vblk[:, bp * NCH * OB + c * OB + hl * 128:
                           bp * NCH * OB + c * OB + (hl + 1) * 128]
                qT = qbT[:, hpar * 128:(hpar + 1) * 128]
                kT = kbT[:, hpar * 128:(hpar + 1) * 128]
                am = asb[:, hpar * 128:(hpar + 1) * 128]
                hp = hpsb[:, hpar * 128:(hpar + 1) * 128]
                hsl = hsb[:, hg * 128:(hg + 1) * 128]
                ecg = ecols[:, (c * 16 + hg) * 2: (c * 16 + hg) * 2 + 1]
                ect = ecols[:, (c * 16 + hg) * 2 + 1: (c * 16 + hg) * 2 + 2]
                mm(pbank[2][:, 0:128], qbs[:, hl * 128:(hl + 1) * 128],
                   ident[:, :], True, True,
                   reads=("qb%d" % par,), writes=("p2",))
                mm(pbank[3][:, 0:128], kbsl[:, hl * 128:(hl + 1) * 128],
                   ident[:, :], True, True, reads=("kbr",), writes=("p3",))
                b.op("dve", lambda e, qT=qT: e.tensor_copy(
                    qT, pbank[2][:, 0:128]), reads=("p2",),
                    writes=("qT%d" % hpar,))
                b.op("dve", lambda e, kT=kT: e.tensor_copy(
                    kT, pbank[3][:, 0:128]), reads=("p3",),
                    writes=("kT%d" % hpar,))
                b.op("dve", lambda e, hp=hp, hsl=hsl, ecg=ecg:
                     e.tensor_single_scalar(hp, hsl, ecg, ALU.mult),
                     reads=("H%d" % blk, "ecols"), writes=("hpp%d" % hpar,))
                mm(pbank[4][:, 0:128], kT, qT, True, True,
                   reads=("kT%d" % hpar, "qT%d" % hpar), writes=("p4",))
                b.op("dve", lambda e, am=am: e.select(
                    am, masku[:, :], pbank[4][:, 0:128], zbf[:, :]),
                    reads=("p4", "consts", "zbf"), writes=("am%d" % hpar,))
                mm(pbank[5][:, 0:128], vsl, am, True, False,
                   reads=("am%d" % hpar, "vbl%d" % bp), writes=("p5",))
                mm(pbank[5][:, 0:128], hp, qT, False, True,
                   reads=("hpp%d" % hpar, "qT%d" % hpar), writes=("p5",))
                mm(pbank[7][:, 0:128], kd[:, hl * 128:(hl + 1) * 128], vsl,
                   True, True,
                   reads=("kd%d" % par, "vbl%d" % bp), writes=("p7",))
                osl = osb[:, hg * 1024 + c * 128: hg * 1024 + (c + 1) * 128]
                b.op("dve", lambda e, osl=osl: e.tensor_copy(
                    osl, pbank[5][:, 0:128]), reads=("p5", "p7"),
                    writes=("osbr",))
                b.op("dve", lambda e, hsl=hsl, ect=ect: e.scalar_tensor_tensor(
                    hsl, hsl, ect, pbank[7][:, 0:128], ALU.mult, ALU.add),
                    reads=("H%d" % blk, "ecols", "p7"), writes=("H%d" % blk,))
            if c == NCH - 1 and blk + 2 < NBLK:
                p4_load(blk + 2)

        pipeline([(0, p4_A), (1, p4_S), (2, p4_Q), (2, p4_K), (1, p4_SB),
                  (3, p4_W)], NBLK * NCH)
        if _DBG:
            dump_src["dosb"] = (osb[:, :], "osbr")
            dump_src["dzbf"] = (zbf[:, :], "zbf")
        issue_wload(12)
        issue_wload(13)
        b.run_block()
        if _MAXPH <= 6:
            do_dump()
            return nc

        es2b.close()
        es2.close()
        es3 = contextlib.ExitStack()
        def sb3(name, shape, dt):
            return es3.enter_context(nc.sbuf_tensor(name, shape, dt, side="right"))
        x1 = sb3("x1b", [128, NCH * 2048], F32)
        biasB = sb3("biasB", [128, 2048], F32)
        es3a = contextlib.ExitStack()
        def sb3a(name, shape, dt):
            return es3a.enter_context(nc.sbuf_tensor(name, shape, dt, side="right"))
        xts = sb3a("xts2", [128, 2048], F32)
        o2st = sb3a("o2st", [128, 1024], BF16)
        rstdrow = sb3a("rstdrow", [1, 1024], F32)
        rhirow = sb3a("rhirow", [1, 1024], BF16)
        rlorow = sb3a("rlorow", [1, 1024], BF16)
        rstdB = sb3a("rstdB", [128, 1024], F32)

        for h in range(NH):
            o2 = o2st[:, 0:1024]
            b.op("act", lambda e, o2=o2, h=h: e.activation(
                o2, osb[:, h * 1024:(h + 1) * 1024], AF.Square),
                reads=("osbr",), writes=("o2s",))
            for half in range(2):
                pk = pbank[half]
                mm(pk[0:1, :], onesc[:, :],
                   o2[:, half * 512:(half + 1) * 512], h == 0, h == NH - 1,
                   reads=("o2s", "consts"), writes=("p%d" % half,))
        for half in range(2):
            pk = pbank[half]
            rr = rstdrow[0:1, half * 512:(half + 1) * 512]
            b.op("act", lambda e, rr=rr, pk=pk: e.activation(
                rr, pk[0:1, :], AF.Ln, bias=EPS_RMS, scale=1.0 / 2048.0),
                reads=("p%d" % half,), writes=("rsr",))
            b.op("act", lambda e, rr=rr: e.activation(
                rr, rr, AF.Exp, scale=-0.5), reads=("rsr",), writes=("rsr",))
        b.op("dve", lambda e: e.tensor_copy(rhirow[0:1, :], rstdrow[0:1, :]),
             reads=("rsr",), writes=("rhi",))
        b.op("dve", lambda e: e.tensor_tensor(
            rlorow[0:1, :], rstdrow[0:1, :], rhirow[0:1, :], ALU.subtract),
            reads=("rsr", "rhi"), writes=("rlo",))
        for half in range(2):
            pk = pbank[2 + half]
            pr = "p%d" % (2 + half)
            mm(pk[:, :], ones1[:, :], rhirow[0:1, half * 512:(half + 1) * 512],
               True, False, reads=("rhi", "consts"), writes=(pr,))
            mm(pk[:, :], ones1[:, :], rlorow[0:1, half * 512:(half + 1) * 512],
               False, True, reads=("rlo",), writes=(pr,))
            b.op("dve", lambda e, pk=pk, half=half: e.tensor_copy(
                rstdB[:, half * 512:(half + 1) * 512], pk[:, :]),
                reads=(pr,), writes=("rstdB",))
        for h in range(NH):
            osl = osb[:, h * 1024:(h + 1) * 1024]
            b.op("dve", lambda e, osl=osl: e.tensor_tensor(
                osl, osl, rstdB[:, :], ALU.mult),
                reads=("osbr", "rstdB"), writes=("osbr",))
            b.op("dve", lambda e, osl=osl, h=h: e.tensor_single_scalar(
                osl, osl, gwcol[:, h:h + 1], ALU.mult),
                reads=("osbr", "consts"), writes=("osbr",))
        for blk in range(NBLK):
            idx = 12 + blk
            ensure_load(12 + blk)
            for t in range(NCH):
                j = blk * NCH + t
                par = j % 2
                pw = pbank[4 + par]
                pres = "p%d" % (4 + par)
                for r in range(HCN):
                    mm(pw[:, :],
                       osb[:, r * 1024 + t * 128: r * 1024 + (t + 1) * 128],
                       wsl(idx, r), r == 0, r == HCN - 1,
                       reads=("wslot%d" % (idx % 2), "osbr"), writes=(pres,))
                x1sl = x1[:, t * 2048 + blk * OB: t * 2048 + (blk + 1) * OB]
                b.op("dve", lambda e, x1sl=x1sl, pw=pw: e.tensor_copy(
                    x1sl, pw[:, :]), reads=(pres,), writes=("x1_%d" % t,))
            ensure_load(12 + blk + 1)
        for t in range(NCH):
            xt = xts[:, 0:2048]
            dma("syq", xt, xin[t * 128:(t + 1) * 128, :], writes=("xtsr",))
            x1sl = x1[:, t * 2048:(t + 1) * 2048]
            b.op("dve", lambda e, x1sl=x1sl, xt=xt: e.tensor_tensor(
                x1sl, x1sl, xt, ALU.add),
                reads=("x1_%d" % t, "xtsr"), writes=("x1_%d" % t,))
        if _DBG:
            dump_src["dx1"] = (x1[:, :], "x1_7")
        ensure_load(17)
        b.run_block()
        if _MAXPH <= 7:
            do_dump()
            return nc

        es3a.close()
        es3b0 = contextlib.ExitStack()
        def sb3b0(name, shape, dt):
            return es3b0.enter_context(nc.sbuf_tensor(name, shape, dt, side="right"))
        biasrow = sb3b0("brb", [1, 2048], F32)
        bhirow = sb3b0("bhirow", [1, 2048], BF16)
        blorow = sb3b0("blorow", [1, 2048], BF16)
        dma("syq", biasrow[:, :], biasd, writes=("biasrow",))

        b.op("dve", lambda e: e.tensor_copy(bhirow[0:1, :], biasrow[0:1, :]),
             reads=("biasrow",), writes=("bhi",))
        b.op("dve", lambda e: e.tensor_tensor(
            blorow[0:1, :], biasrow[0:1, :], bhirow[0:1, :], ALU.subtract),
            reads=("biasrow", "bhi"), writes=("blo",))
        for q in range(4):
            pk = pbank[q % 2]
            pr = "p%d" % (q % 2)
            mm(pk[:, :], ones1[:, :], bhirow[0:1, q * 512:(q + 1) * 512],
               True, False, reads=("bhi", "consts"), writes=(pr,))
            mm(pk[:, :], ones1[:, :], blorow[0:1, q * 512:(q + 1) * 512],
               False, True, reads=("blo",), writes=(pr,))
            b.op("dve", lambda e, pk=pk, q=q: e.tensor_copy(
                biasB[:, q * 512:(q + 1) * 512], pk[:, :]),
                reads=(pr,), writes=("biasB",))
        b.run_block()
        if _MAXPH <= 8:
            do_dump()
            return nc
        es3b0.close()
        es3b = contextlib.ExitStack()
        def sb3b(name, shape, dt):
            return es3b.enter_context(nc.sbuf_tensor(name, shape, dt, side="right"))
        xsq = sb3b("xsq2", [128, 2048], F32)
        ltst = sb3b("ltst2", [128, 2 * 2048], BF16)
        outst = sb3b("outst", [128, 2 * OB], F32)

        for t in range(NCH):
            emit_ln(t, None, "lnr")
        b.run_block()
        if _MAXPH <= 9:
            do_dump()
            return nc

        for blk in range(NBLK):
            idx = 16 + blk
            ensure_load(16 + blk)
            for t in range(NCH):
                j = blk * NCH + t
                par = j % 2
                pf = pbank[2 + par]
                pres = "p%d" % (2 + par)
                for ic in range(HCN):
                    mm(pf[:, :],
                       ln[:, ic * 1024 + t * 128: ic * 1024 + (t + 1) * 128],
                       wsl(idx, ic), ic == 0, ic == HCN - 1,
                       reads=("wslot%d" % (idx % 2), "lnr"), writes=(pres,))
                ost = outst[:, par * OB:(par + 1) * OB]
                b.op("dve", lambda e, ost=ost, pf=pf, blk=blk:
                     e.scalar_tensor_tensor(
                         ost, pf[:, :], 1.0,
                         biasB[:, blk * OB:(blk + 1) * OB], ALU.mult, ALU.add),
                     reads=(pres, "biasB"), writes=("ost%d" % par,))
                b.op("dve", lambda e, ost=ost, t=t, blk=blk: e.tensor_tensor(
                    ost, ost,
                    x1[:, t * 2048 + blk * OB: t * 2048 + (blk + 1) * OB],
                    ALU.add),
                    reads=("ost%d" % par, "x1_%d" % t), writes=("ost%d" % par,))
                dma("syq",
                    outd[t * 128:(t + 1) * 128, blk * OB:(blk + 1) * OB],
                    ost, reads=("ost%d" % par,))
            ensure_load(16 + blk + 1)
        vfin = b.cnt["syq"]
        b.prog["syq"].append(
            lambda e, s=b.sems["syq"], v=vfin: e.wait_ge(s, v))
        b.run_block()
        if _MAXPH <= 99:
            do_dump()
        es3b.close()
        es3.close()

    return nc


_NC_CACHE = {}
_LAST_RES = None


def _get_nc():
    if "nc" not in _NC_CACHE:
        _NC_CACHE["nc"] = build_nc()
    return _NC_CACHE["nc"]


def kernel(L0_attn_x, L0_attn_Wq, L0_attn_Wf, L0_attn_Wi, L0_attn_Wo,
           L0_attn_g_weight, L0_ffn_weight, L0_ffn_bias):
    import ml_dtypes
    BFD = ml_dtypes.bfloat16
    x = np.ascontiguousarray(
        np.asarray(L0_attn_x, dtype=np.float32).reshape(8192, 2048))
    Wq = np.ascontiguousarray(np.asarray(L0_attn_Wq, dtype=np.float32))
    Wf = np.ascontiguousarray(np.asarray(L0_attn_Wf, dtype=np.float32))
    Wi = np.ascontiguousarray(np.asarray(L0_attn_Wi, dtype=np.float32))
    Wo = np.ascontiguousarray(np.asarray(L0_attn_Wo, dtype=np.float32))
    gw = np.asarray(L0_attn_g_weight, dtype=np.float32)
    Wn = np.ascontiguousarray(np.asarray(L0_ffn_weight, dtype=np.float32))
    bias = np.asarray(L0_ffn_bias, dtype=np.float32)

    ident = np.eye(128, dtype=np.float32).astype(BFD)
    tau = np.arange(128)[:, None]
    tcol = np.arange(128)[None, :]
    lcm = ((tau <= tcol).astype(np.float32)
           - ((tau <= 63).astype(np.float32)
              * np.ones((1, 128), np.float32))).astype(BFD)
    u2m = np.concatenate(
        [(np.arange(128) <= 63).astype(np.float32)[:, None],
         np.ones((128, 1), np.float32)], 1).astype(BFD)
    u64 = (np.arange(128) >= 64).astype(np.float32)[:, None].astype(BFD)
    masku = (tau <= tcol).astype(np.uint8)
    ones1 = np.ones((1, 128), np.float32).astype(BFD)
    onesc = np.ones((128, 1), np.float32).astype(BFD)
    sel8 = np.zeros((8, 1024), np.float32)
    for c in range(8):
        sel8[c, c * 128:(c + 1) * 128] = 1.0
    sel8 = sel8.astype(BFD)
    gwcol = np.ascontiguousarray(gw.reshape(16, 128).T)
    biasr = np.ascontiguousarray(bias.reshape(1, 2048))

    in_maps = []
    for r in range(NCORE):
        pm = np.zeros((128, 8), np.float32)
        pidx = np.zeros((1, 8), np.int32)
        for j in range(1, 8):
            p = r - j
            if p >= 0:
                pm[:, j] = 1.0
                pidx[0, j] = p
        in_maps.append(dict(
            xin=np.ascontiguousarray(x[r * T:(r + 1) * T]),
            wf=Wf, wi=Wi, wq=Wq, wo=Wo,
            wns=np.ascontiguousarray(Wn[r * 256:(r + 1) * 256]),
            biasr=biasr, gwcol=gwcol,
            identin=ident, lcin=lcm, u2in=u2m, u64in=u64,
            maskin=masku, ones1in=ones1, onescin=onesc, sel8in=sel8,
            pmaskin=pm, pidxin=pidx,
            pltrin=np.tile((np.arange(8) < r).astype(np.float32)[None, :],
                           (128, 1)),
            cstin=np.tile(np.array([[1.0, LOG_DKSC, EPS_LN, 0.0]],
                                   np.float32), (128, 1)),
        ))
    nc = _get_nc()
    res = run_bass_kernel_spmd(nc, in_maps, core_ids=list(range(NCORE)))
    global _LAST_RES
    _LAST_RES = res.results
    out = np.concatenate([res.results[r]["out"] for r in range(NCORE)], axis=0)
    return np.ascontiguousarray(out.reshape(1, 8192, 2048), dtype=np.float32)


if __name__ == "__main__":
    nc = build_nc()
    n = sum(1 for _ in nc.all_instructions())
    print("built ok, instructions:", n)


# version 12 (final)
# speedup vs baseline: 1.0431x; 1.0074x over previous
import contextlib
import os
import numpy as np
import concourse.bass as bass
import concourse.mybir as mybir
from concourse.bass_utils import run_bass_kernel_spmd

F32 = mybir.dt.float32
BF16 = mybir.dt.bfloat16
U8 = mybir.dt.uint8
I32 = mybir.dt.int32
AF = mybir.ActivationFunctionType
ALU = mybir.AluOpType
AX = mybir.AxisListType.X

T = 1024
NCH = 8
NBLK = 4
OB = 512
HCN = 16
NH = 16
NHB = 4
NCORE = 8
LOG_DKSC = float(-0.5 * np.log(128.0))
EPS_LN = 1e-5
EPS_RMS = 1e-5

ENG = ("pe", "act", "dve", "syq", "gpq")
ENG2SEC = {"pe": "tensor", "act": "scalar", "dve": "vector",
           "syq": "sync", "gpq": "gpsimd"}
CENG = ("pe", "act", "dve")
PSUMRES = frozenset("p%d" % i for i in range(8))


class B:

    def __init__(self, nc, es):
        self.nc = nc
        self.es = es
        self.prog = {k: [] for k in ENG}
        self.sems = {k: es.enter_context(nc.semaphore("s_" + k)) for k in ENG}
        self.xsems = {}
        self.cnt = {k: 0 for k in ENG}
        self.hwm = {k: {} for k in ENG}
        self.res = {}

    def xsem(self, name):
        if name not in self.xsems:
            self.xsems[name] = self.es.enter_context(self.nc.semaphore(name))
        return self.xsems[name]

    def _wait(self, eng, semkey, val):
        if semkey in ("syq", "gpq"):
            val = self.cnt[semkey]
        if val <= 0 or self.hwm[eng].get(semkey, 0) >= val:
            return
        self.hwm[eng][semkey] = val
        s = self.sems.get(semkey) or self.xsems[semkey]
        self.prog[eng].append(lambda e, s=s, v=val: e.wait_ge(s, v))

    def op(self, eng, fn, reads=(), writes=(), waits=(), inc=None):
        if inc is None:
            inc = 16 if eng in ("syq", "gpq") else 1
        reads = tuple(reads)
        writes = tuple(writes)
        excl = tuple(r for r in reads if r in PSUMRES)
        if excl:
            reads = tuple(r for r in reads if r not in PSUMRES)
            writes = writes + excl
        need = list(waits)
        for r in reads:
            g = self.res.get(r)
            if g and g["w"] is not None:
                we = g["w"][0]
                if we != eng or eng != "pe":
                    need.append(g["w"])
        for w in writes:
            g = self.res.get(w)
            if g:
                if g["w"] is not None:
                    we = g["w"][0]
                    if we != eng or eng in ("act", "dve", "syq", "gpq"):
                        need.append(g["w"])
                for t in g["r"]:
                    if t[0] != eng or eng in ("syq", "gpq"):
                        need.append(t)
        for semkey, val in need:
            self._wait(eng, semkey, val)
        self.cnt[eng] += inc
        tag = (eng, self.cnt[eng])
        s = self.sems[eng]
        self.prog[eng].append(lambda e, fn=fn, s=s, inc=inc:
                              fn(e).then_inc(s, inc))
        for r in reads:
            self.res.setdefault(r, {"w": None, "r": []})["r"].append(tag)
        for w in writes:
            self.res[w] = {"w": tag, "r": []}
        return tag

    def mark_write(self, resname, semkey, val):
        self.res[resname] = {"w": (semkey, val), "r": []}

    def barrier(self):
        for g in self.res.values():
            if g["w"] is not None and g["w"][0] in CENG:
                g["w"] = None
            g["r"] = [t for t in g["r"] if t[0] not in CENG]

    def run_block(self):
        nc = self.nc
        prog = {k: list(v) for k, v in self.prog.items()}
        for k in ENG:
            self.prog[k] = []
        with nc.Block() as blk:
            for k in ENG:
                if not prog[k]:
                    continue
                dec = getattr(blk, ENG2SEC[k])
                def make(ops):
                    def body(eng):
                        for f in ops:
                            f(eng)
                    return body
                dec(make(prog[k]))
        self.barrier()


def pipeline(stages, n):
    maxlag = max(l for l, _ in stages)
    for it in range(n + maxlag):
        for lag, fn in stages:
            i = it - lag
            if 0 <= i < n:
                fn(i)


_MAXPH = int(os.environ.get("KMAXPH", "99"))
_P2N = int(os.environ.get("KP2N", "32"))
_NOCOLL = int(os.environ.get("KNOCOLL", "0"))
_STAGES = os.environ.get("KSTAGES", "ACGTBEKD")
_DBG = int(os.environ.get("KDBG", "0"))


def build_nc():
    nc = bass.Bass("TRN2", target_bir_lowering=False)

    xin = nc.dram_tensor("xin", [T, 2048], F32, kind="ExternalInput").ap()
    wdr = [nc.dram_tensor(nm, [2048, 2048], F32, kind="ExternalInput").ap()
           for nm in ("wf", "wi", "wq", "wo")]
    wns = nc.dram_tensor("wns", [256, 2048], F32, kind="ExternalInput").ap()
    biasd = nc.dram_tensor("biasr", [1, 2048], F32, kind="ExternalInput").ap()
    gwd = nc.dram_tensor("gwcol", [128, 16], F32, kind="ExternalInput").ap()
    identd = nc.dram_tensor("identin", [128, 128], BF16, kind="ExternalInput").ap()
    lcd = nc.dram_tensor("lcin", [128, 128], BF16, kind="ExternalInput").ap()
    u2d = nc.dram_tensor("u2in", [128, 2], BF16, kind="ExternalInput").ap()
    u64d = nc.dram_tensor("u64in", [128, 1], BF16, kind="ExternalInput").ap()
    maskd = nc.dram_tensor("maskin", [128, 128], U8, kind="ExternalInput").ap()
    ones1d = nc.dram_tensor("ones1in", [1, 128], BF16, kind="ExternalInput").ap()
    onescd = nc.dram_tensor("onescin", [128, 1], BF16, kind="ExternalInput").ap()
    sel8d = nc.dram_tensor("sel8in", [8, 1024], BF16, kind="ExternalInput").ap()
    pmaskd = nc.dram_tensor("pmaskin", [128, 8], F32, kind="ExternalInput").ap()
    pidxd = nc.dram_tensor("pidxin", [1, 8], I32, kind="ExternalInput").ap()
    pltrd = nc.dram_tensor("pltrin", [128, 8], F32, kind="ExternalInput").ap()
    cstd = nc.dram_tensor("cstin", [128, 4], F32, kind="ExternalInput").ap()
    outd = nc.dram_tensor("out", [T, 2048], F32, kind="ExternalOutput").ap()
    dbgd = {}
    if _DBG:
        for nm, shape, dt in (("dln", [128, 16384], BF16),
                              ("dkb", [128, 16384], BF16),
                              ("dosb", [128, 16384], BF16),
                              ("dhsb", [128, 2048], F32),
                              ("deD", [T, 2048], BF16),
                              ("dvD", [T, 2048], BF16),
                              ("drows", [8, 2048], BF16),
                              ("decols", [128, 256], F32),
                              ("dgcol", [128, 16], F32),
                              ("dzbf", [128, 128], BF16),
                              ("dA", [128, 128], BF16),
                              ("dqT", [128, 128], BF16),
                              ("dkT", [128, 128], BF16),
                              ("dx1", [128, 16384], F32)):
            dbgd[nm] = nc.dram_tensor(nm, shape, dt, kind="ExternalOutput").ap()

    eD = nc.dram_tensor("eD", [T, 2048], BF16).ap()
    vD = nc.dram_tensor("vD", [T, 2048], BF16).ap()
    gin = nc.dram_tensor("gin", [2048, 128], F32).ap()
    gin2 = nc.dram_tensor("gin2", [128, 16], F32).ap()
    goutb_h = [nc.dram_tensor("goutb%d" % i, [NCORE * 512, 128], F32)
               for i in range(4)]
    gout2_h = nc.dram_tensor("gout2", [NCORE * 128, 16], F32)
    ginq = nc.dram_tensor("ginq", [2048, 256], BF16).ap()
    goutq = nc.dram_tensor("goutq", [NCORE * 2048, 256], BF16).ap()
    goutb = [h.ap() for h in goutb_h]
    gout2 = gout2_h.ap()

    es = contextlib.ExitStack()
    with es:
        def sbt(name, shape, dt):
            return es.enter_context(nc.sbuf_tensor(name, shape, dt))

        ident = sbt("identb", [128, 128], BF16)
        lc = sbt("lcb", [128, 128], BF16)
        u2 = sbt("u2b", [128, 2], BF16)
        u64c = sbt("u64b", [128, 1], BF16)
        masku = sbt("maskb", [128, 128], U8)
        ones1 = sbt("ones1b", [1, 128], BF16)
        onesc = sbt("onescb", [128, 1], BF16)
        sel8 = sbt("sel8b", [8, 1024], BF16)
        gwcol = sbt("gwb", [128, 16], F32)
        pmask = sbt("pmb", [128, 8], F32)
        pidx = sbt("pib", [1, 8], I32)
        pltr = sbt("pltrb", [128, 8], F32)
        zbf = sbt("zbf", [128, 128], BF16)
        cstb = sbt("cstb", [128, 4], F32)
        nc.const_aps.register(F32, 1.0, cstb[:, 0:1])
        nc.const_aps.register(F32, LOG_DKSC, cstb[:, 1:2])
        nc.const_aps.register(F32, EPS_LN, cstb[:, 2:3])
        ln = sbt("ln", [128, HCN * 1024], BF16)
        osb = sbt("osb", [128, NH * 1024], BF16)
        wst = sbt("wst", [128, 2 * HCN * OB], BF16)
        rows8 = sbt("rows8", [8, 2048], BF16)
        ecols = sbt("ecols", [128, 256], F32)
        gcol = sbt("gcolb", [128, 16], F32)
        hsb = sbt("hsb", [128, NH * 128], F32)
        st1 = sbt("st1", [128, 2], F32)
        st2 = sbt("st2", [128, 2], F32)
        smean = sbt("smean", [128, 2], F32)
        svar = sbt("svar", [128, 2], F32)
        srstd = sbt("srstd", [128, 2], F32)
        sm2 = sbt("sm2", [128, 2], F32)

        es1 = contextlib.ExitStack()
        def sb1(name, shape, dt):
            return es1.enter_context(nc.sbuf_tensor(name, shape, dt, side="right"))
        wq32 = sb1("wq32", [128, 2 * 2048], F32)
        awb = sb1("awb", [128, 2048], F32)
        sgnb = sb1("sgnb", [128, 2048], BF16)
        mskq = sb1("mskqb", [128, 2048], BF16)
        qtb = sb1("qtb", [128, 2048], BF16)
        qstage = sb1("qstage", [128, HCN * 128], BF16)
        thr = sb1("thrb", [128, 1], F32)
        xts = sb1("xts", [128, 2 * 2048], F32)
        xsq = sb1("xsq", [128, 2048], F32)
        ltst = sb1("ltst", [128, 2 * 2048], BF16)
        pbank = [es.enter_context(nc.psum_tensor("pb%d" % i, [128, 512], F32))
                 for i in range(8)]

        rgH = es.enter_context(nc.gpsimd.register("rgH"))
        rgG = es.enter_context(nc.gpsimd.register("rgG"))
        b = B(nc, es)
        dump_src = {}

        def do_dump():
            if not _DBG:
                return
            for nm, (srcap, res) in list(dump_src.items()):
                dma("syq", dbgd[nm], srcap, reads=(res,))
            vv = b.cnt["syq"]
            b.prog["syq"].append(
                lambda e, s=b.sems["syq"], v=vv: e.wait_ge(s, v))
            b.run_block()
        for s in ("cc_q", "cc_h", "cc_g"):
            b.xsem(s)

        def dma(q, dst, src, reads=(), writes=()):
            return b.op(q, lambda e, d=dst, s=src: e.dma_start(d, s),
                        reads=reads, writes=writes)

        def mm(out_ap, lhsT, rhs, start, stop, reads=(), writes=()):
            return b.op("pe", lambda e, o=out_ap, l=lhsT, r=rhs, st=start,
                        sp=stop: e.matmul(o, l, r, start=st, stop=sp),
                        reads=reads, writes=writes)

        def wsl(idx, hc):
            s0 = (idx % 2) * HCN * OB
            return wst[:, s0 + hc * OB: s0 + (hc + 1) * OB]

        def issue_wload(idx):
            blk = idx % 4
            slot = "wslot%d" % (idx % 2)
            s0 = (idx % 2) * HCN * OB
            if idx < 16:
                src = wdr[idx // 4][:, blk * OB:(blk + 1) * OB].rearrange(
                    "(hc p) o -> p hc o", p=128)
                dst = wst[:, s0:s0 + HCN * OB].rearrange(
                    "p (hc o) -> p hc o", hc=HCN)
                dma("gpq", dst, src, writes=(slot,))
            else:
                full = wst[:, s0:s0 + HCN * OB].rearrange(
                    "p (ic o) -> p ic o", o=OB)
                for half in range(2):
                    rank = blk * 2 + half
                    src = goutq[rank * 2048:(rank + 1) * 2048, :].rearrange(
                        "(ic p) o -> p ic o", p=128)
                    dma("gpq", full[:, :, half * 256:(half + 1) * 256], src,
                        reads=("goutq",), writes=(slot,))

        _wload_next = [0]

        def ensure_load(upto):
            while _wload_next[0] <= min(upto, 19):
                issue_wload(_wload_next[0])
                _wload_next[0] += 1

        def proj_mms(idx, c, pk, pres):
            for hc in range(HCN):
                lnsl = ln[:, hc * 1024 + c * 128: hc * 1024 + (c + 1) * 128]
                mm(pk[:, :], lnsl, wsl(idx, hc), hc == 0, hc == HCN - 1,
                   reads=("wslot%d" % (idx % 2), "lnr%d" % c), writes=(pres,))

        def emit_coll(kind, ins, outs, semname, waits=()):
            s = b.xsem(semname)
            if _NOCOLL:
                b.prog["gpq"].append(
                    lambda e, s=s: e.engine_nop().then_inc(s, 1))
                return
            for semkey, val in waits:
                b._wait("gpq", semkey, val)
            def f(e, s=s, ins=ins, outs=outs, kind=kind):
                e.collective_compute(
                    kind, ALU.bypass, replica_groups=[list(range(NCORE))],
                    ins=ins, outs=outs).then_inc(s, 1)
            b.prog["gpq"].append(f)

        for dst, src in ((ident, identd), (lc, lcd), (u2, u2d), (u64c, u64d),
                         (masku, maskd), (ones1, ones1d), (onesc, onescd),
                         (sel8, sel8d), (gwcol, gwd), (pmask, pmaskd),
                         (pidx, pidxd), (pltr, pltrd), (cstb, cstd)):
            dma("syq", dst[:, :], src, writes=("consts",))
        for rc in range(2):
            dma("syq", wq32[:, rc * 2048:(rc + 1) * 2048],
                wns[rc * 128:(rc + 1) * 128, :], writes=("wq32_%d" % rc,))
        ensure_load(1)
        b.op("dve", lambda e: e.memset(zbf[:, :], 0.0), writes=("zbf",))
        b.op("dve", lambda e: e.memset(gcol[:, :], 0.0), writes=("gcol",))
        b.op("dve", lambda e: e.memset(hsb[:, :], 0.0),
             writes=("H0", "H1", "H2", "H3"))
        b.run_block()
        if _MAXPH <= 1:
            do_dump()
            return nc

        def emit_ln(t, src_dram, tag):
            par = t % 2
            xq = xsq[:, 0:2048]
            lt = ltst[:, par * 2048:(par + 1) * 2048]
            c1 = st1[:, par:par + 1]
            c2 = st2[:, par:par + 1]
            cm = smean[:, par:par + 1]
            cv = svar[:, par:par + 1]
            cr = srstd[:, par:par + 1]
            c5 = sm2[:, par:par + 1]
            R1, R2, RM, RV, RR, R5 = ("st1%d" % par, "st2%d" % par,
                                      "smean%d" % par, "svar%d" % par,
                                      "srstd%d" % par, "sm2%d" % par)
            if src_dram is not None:
                xt = xts[:, par * 2048:(par + 1) * 2048]
                xr = ("xts%d" % par,)
                dma("syq", xt, src_dram[t * 128:(t + 1) * 128, :], writes=xr)
            else:
                xt = x1[:, t * 2048:(t + 1) * 2048]
                xr = ("x1_%d" % t,)
            b.op("dve", lambda e, xt=xt, c1=c1: e.reduce_sum(c1, xt, axis=AX),
                 reads=xr, writes=(R1,))
            b.op("act", lambda e, xt=xt, xq=xq, c2=c2: e.activation(
                xq, xt, AF.Square, accum_out=c2),
                reads=xr, writes=("xsq", R2))
            b.op("dve", lambda e, c1=c1, cm=cm: e.tensor_scalar_mul(
                cm, c1, 1.0 / 2048.0), reads=(R1,), writes=(RM,))
            b.op("dve", lambda e, c2=c2, cv=cv: e.tensor_scalar_mul(
                cv, c2, 1.0 / 2048.0), reads=(R2,), writes=(RV,))
            b.op("dve", lambda e, cm=cm, c5=c5: e.tensor_tensor(
                c5, cm, cm, ALU.mult), reads=(RM,), writes=(R5,))
            b.op("dve", lambda e, cv=cv, c5=c5: e.tensor_tensor(
                cv, cv, c5, ALU.subtract), reads=(RV, R5), writes=(RV,))
            b.op("act", lambda e, cv=cv: e.activation(
                cv, cv, AF.Ln, bias=EPS_LN), reads=(RV,), writes=(RV,))
            b.op("act", lambda e, cv=cv, cr=cr: e.activation(
                cr, cv, AF.Exp, scale=-0.5), reads=(RV,), writes=(RR,))
            b.op("dve", lambda e, xt=xt, lt=lt, cm=cm, cr=cr: e.tensor_scalar(
                lt, xt, cm, cr, ALU.subtract, ALU.mult),
                reads=xr + (RM, RR), writes=("ltst%d" % par,))
            for hc in range(HCN):
                pk = pbank[6 + (hc % 2)]
                pr = "p%d" % (6 + hc % 2)
                mm(pk[:, 0:128], lt[:, hc * 128:(hc + 1) * 128], ident[:, :],
                   True, True, reads=("ltst%d" % par, "consts"), writes=(pr,))
                b.op("dve", lambda e, pk=pk, hc=hc, t=t: e.tensor_copy(
                    ln[:, hc * 1024 + t * 128: hc * 1024 + (t + 1) * 128],
                    pk[:, 0:128]), reads=(pr,), writes=("lnr%d" % t,))

        for t in range(NCH):
            emit_ln(t, xin, "lnr")
        for rc in range(2):
            wq_ = wq32[:, rc * 2048:(rc + 1) * 2048]
            rrc = ("wq32_%d" % rc,)
            b.op("dve", lambda e, w=wq_: e.reduce_sum(
                thr[:, :], w, axis=AX, apply_absolute_value=True),
                reads=rrc, writes=("thr",))
            b.op("dve", lambda e: e.tensor_scalar_mul(
                thr[:, :], thr[:, :], 0.7 / 2048.0),
                reads=("thr",), writes=("thr",))
            b.op("act", lambda e, w=wq_: e.activation(awb[:, :], w, AF.Abs),
                 reads=rrc, writes=("awq",))
            b.op("act", lambda e, w=wq_: e.activation(sgnb[:, :], w, AF.Sign),
                 reads=rrc, writes=("sgnq",))
            b.op("dve", lambda e: e.tensor_single_scalar(
                mskq[:, :], awb[:, :], thr[:, :], ALU.is_gt),
                reads=("awq", "thr"), writes=("mskq",))
            b.op("dve", lambda e: e.tensor_tensor(
                qtb[:, :], sgnb[:, :], mskq[:, :], ALU.mult),
                reads=("sgnq", "mskq"), writes=("qtb",))
            for ic in range(HCN):
                pk = pbank[6 + (ic % 2)]
                pr = "p%d" % (6 + ic % 2)
                mm(pk[:, 0:128], qtb[:, ic * 128:(ic + 1) * 128], ident[:, :],
                   True, True, reads=("qtb", "consts"), writes=(pr,))
                b.op("dve", lambda e, pk=pk, ic=ic: e.tensor_copy(
                    qstage[:, ic * 128:(ic + 1) * 128], pk[:, 0:128]),
                    reads=(pr,), writes=("qstage",))
            dma("syq",
                ginq[:, rc * 128:(rc + 1) * 128].rearrange(
                    "(ic p) o -> p ic o", p=128),
                qstage.rearrange("p (ic o) -> p ic o", o=128),
                reads=("qstage",), writes=("ginq",))

        if _DBG:
            dump_src["dln"] = (ln[:, :], "lnr")

        es1.close()
        es2 = contextlib.ExitStack()
        def sb2(name, shape, dt):
            return es2.enter_context(nc.sbuf_tensor(name, shape, dt, side="right"))
        kb = sb2("kbt", [128, NCH * 2048], BF16)
        es2a = contextlib.ExitStack()
        def sb2a(name, shape, dt):
            return es2a.enter_context(nc.sbuf_tensor(name, shape, dt, side="right"))
        ta = sb2a("ta", [128, 3 * OB], F32)
        te = sb2a("te", [128, 3 * OB], F32)
        tf = sb2a("tf", [128, 3 * OB], BF16)
        fmn = sb2a("fmn", [128, 3 * OB], F32)
        g32 = sb2a("g32", [128, 3 * OB], F32)
        ghi = sb2a("ghi", [128, 3 * OB], BF16)
        glo = sb2a("glo", [128, 3 * OB], BF16)
        est = sb2a("est", [128, 3 * OB], BF16)
        srst = sb2a("srst", [1, 3 * OB], BF16)

        emit_coll("AllGather", [ginq.opt()], [goutq.opt()], "cc_q",
                  waits=[("syq", b.cnt["syq"])])
        b.mark_write("goutq", "cc_q", 1)

        FB = (0, 1, 6)
        GB = (2, 3, 7)

        def p2_A(i):
            blk, c = divmod(i, NCH)
            if c == 0:
                ensure_load(blk)
            proj_mms(blk, c, pbank[FB[i % 3]], "p%d" % FB[i % 3])
            if c == NCH - 1:
                ensure_load(blk + 1)

        def p2_C(i):
            par = i % 3
            fres = "p%d" % FB[par]
            taa = ta[:, par * OB:(par + 1) * OB]
            fp = pbank[FB[par]]
            b.op("act", lambda e, fp=fp, taa=taa: e.activation(
                taa, fp[:, :], AF.Abs), reads=(fres,), writes=("ta%d" % par,))
            b.op("act", lambda e, taa=taa: e.activation(
                taa, taa, AF.Exp, scale=-1.0),
                reads=("ta%d" % par,), writes=("ta%d" % par,))
            b.op("act", lambda e, taa=taa: e.activation(
                taa, taa, AF.Ln, bias=1.0),
                reads=("ta%d" % par,), writes=("ta%d" % par,))

        def p2_G(i):
            par = i % 3
            fres = "p%d" % FB[par]
            fp = pbank[FB[par]]
            fmm = fmn[:, par * OB:(par + 1) * OB]
            gg = g32[:, par * OB:(par + 1) * OB]
            gh = ghi[:, par * OB:(par + 1) * OB]
            gl = glo[:, par * OB:(par + 1) * OB]
            taa = ta[:, par * OB:(par + 1) * OB]
            b.op("dve", lambda e, fp=fp, fmm=fmm: e.tensor_scalar_min(
                fmm, fp[:, :], 0.0), reads=(fres,), writes=("fmn%d" % par,))
            b.op("dve", lambda e, fmm=fmm, gg=gg, taa=taa: e.tensor_tensor(
                gg, fmm, taa, ALU.subtract),
                reads=("fmn%d" % par, "ta%d" % par), writes=("g%d" % par,))
            b.op("dve", lambda e, gg=gg, gh=gh: e.tensor_copy(gh, gg),
                 reads=("g%d" % par,), writes=("gh%d" % par,))
            b.op("dve", lambda e, gg=gg, gh=gh, gl=gl: e.tensor_tensor(
                gl, gg, gh, ALU.subtract),
                reads=("g%d" % par, "gh%d" % par), writes=("gl%d" % par,))

        def p2_T(i):
            par = i % 3
            tee = te[:, par * OB:(par + 1) * OB]
            gg = g32[:, par * OB:(par + 1) * OB]
            b.op("act", lambda e, gg=gg, tee=tee: e.activation(
                tee, gg, AF.Exp), reads=("g%d" % par,), writes=("te%d" % par,))

        def p2_B(i):
            par = i % 3
            gh = ghi[:, par * OB:(par + 1) * OB]
            gl = glo[:, par * OB:(par + 1) * OB]
            gp = pbank[GB[par]]
            gres = "p%d" % GB[par]
            mm(gp[:, :], lc[:, :], gh, True, False,
               reads=("gh%d" % par, "consts"), writes=(gres,))
            mm(gp[:, :], lc[:, :], gl, False, True,
               reads=("gl%d" % par,), writes=(gres,))
            rp = pbank[4]
            mm(rp[0:1, :], u64c[:, :], gh, True, False,
               reads=("gh%d" % par,), writes=("p4",))
            mm(rp[0:1, :], u64c[:, :], gl, False, True,
               reads=("gl%d" % par,), writes=("p4",))
            cp = pbank[5]
            for hl in range(NHB):
                mm(cp[:, hl * 2: hl * 2 + 2],
                   gh[:, hl * 128:(hl + 1) * 128], u2[:, :], True, False,
                   reads=("gh%d" % par,), writes=("p5",))
                mm(cp[:, hl * 2: hl * 2 + 2],
                   gl[:, hl * 128:(hl + 1) * 128], u2[:, :], False, True,
                   reads=("gl%d" % par,), writes=("p5",))

        def p2_E(i):
            blk, c = divmod(i, NCH)
            par = i % 3
            gp = pbank[GB[par]]
            gres = "p%d" % GB[par]
            ee = est[:, par * OB:(par + 1) * OB]
            tff = tf[:, par * OB:(par + 1) * OB]
            sr = srst[0:1, par * OB:(par + 1) * OB]
            ecsl = ecols[:, (c * 16 + blk * 4) * 2: (c * 16 + blk * 4) * 2 + 8]
            b.op("act", lambda e, gp=gp, ee=ee: e.activation(
                ee, gp[:, :], AF.Exp, bias=LOG_DKSC),
                reads=(gres,), writes=("est%d" % par,))
            b.op("act", lambda e, gp=gp, tff=tff: e.activation(
                tff, gp[:, :], AF.Exp, scale=-1.0),
                reads=(gres,), writes=("tf%d" % par,))
            b.op("act", lambda e, sr=sr: e.activation(
                sr, pbank[4][0:1, :], AF.Exp),
                reads=("p4",), writes=("sr%d" % par,))
            b.op("act", lambda e, ecsl=ecsl: e.activation(
                ecsl, pbank[5][:, 0:8], AF.Exp),
                reads=("p5",), writes=("ecols",))

        def p2_K(i):
            blk, c = divmod(i, NCH)
            par = i % 3
            tee = te[:, par * OB:(par + 1) * OB]
            tff = tf[:, par * OB:(par + 1) * OB]
            kbsl = kb[:, c * 2048 + blk * OB: c * 2048 + (blk + 1) * OB]
            b.op("dve", lambda e, tee=tee: e.tensor_scalar(
                tee, tee, -1.0, -1.0, ALU.add, ALU.mult),
                reads=("te%d" % par,), writes=("te%d" % par,))
            b.op("dve", lambda e, tee=tee, tff=tff, kbsl=kbsl: e.tensor_tensor(
                kbsl, tee, tff, ALU.mult),
                reads=("te%d" % par, "tf%d" % par), writes=("kbr",))
            gcsl = gcol[:, blk * 4:(blk + 1) * 4]
            cp = pbank[5]
            b.op("dve", lambda e, gcsl=gcsl, cp=cp: e.tensor_tensor(
                gcsl, gcsl, cp[:, 1:8:2], ALU.add),
                reads=("gcol", "p5"), writes=("gcol",))

        def p2_D(i):
            blk, c = divmod(i, NCH)
            par = i % 3
            ee = est[:, par * OB:(par + 1) * OB]
            sr = srst[0:1, par * OB:(par + 1) * OB]
            dma("syq", eD[c * 128:(c + 1) * 128, blk * OB:(blk + 1) * OB],
                ee, reads=("est%d" % par,), writes=("eDram",))
            dma("gpq", rows8[c:c + 1, blk * OB:(blk + 1) * OB], sr,
                reads=("sr%d" % par,), writes=("rows8",))

        _all = [("A", 0, p2_A), ("C", 1, p2_C), ("G", 1, p2_G),
                ("T", 2, p2_T), ("E", 3, p2_E), ("K", 3, p2_K),
                ("D", 3, p2_D), ("B", 2, p2_B)]
        pipeline([(l, f) for nm, l, f in _all if nm in _STAGES], _P2N)
        ensure_load(5)
        if _DBG:
            dump_src["dkb"] = (kb[:, :], "kbr")
            dump_src["deD"] = (eD, "eDram")
            dump_src["drows"] = (rows8[:, :], "rows8")
            dump_src["decols"] = (ecols[:, :], "ecols")
            dump_src["dgcol"] = (gcol[:, :], "gcol")
        b.run_block()
        if _MAXPH <= 3:
            do_dump()
            return nc

        es2a.close()
        es2b = contextlib.ExitStack()
        def sb2b(name, shape, dt):
            return es2b.enter_context(nc.sbuf_tensor(name, shape, dt, side="right"))
        vblk = sb2b("vblk", [128, 2 * NCH * OB], BF16)
        eblk = sb2b("eblk", [128, 2 * NCH * OB], BF16)
        kdst = sb2b("kdst", [128, 2 * OB], BF16)
        qst = sb2b("qstb", [128, 2 * OB], BF16)
        qbst = sb2b("qbst", [128, 2 * OB], BF16)
        qbT = sb2b("qbTb", [128, 2 * 128], BF16)
        kbT = sb2b("kbTb", [128, 2 * 128], BF16)
        asb = sb2b("asbb", [128, 2 * 128], BF16)
        hpsb = sb2b("hpsb", [128, 2 * 128], BF16)
        hp2 = sb2b("hp2", [128, 2 * 512], F32)
        g2all = sb2b("g2all", [128, 128], F32)
        sall = sb2b("sall", [128, 128], F32)
        smy = sb2b("smy", [128, 16], F32)
        sclall = sb2b("sclall", [128, 128], F32)
        tdif = sb2b("tdif", [128, 16], F32)

        dma("syq", gin2[:, :], gcol[:, :], reads=("gcol",), writes=("gin2",))
        emit_coll("AllGather", [gin2.opt()], [gout2.opt()], "cc_g",
                  waits=[("syq", b.cnt["syq"])])
        b.mark_write("gout2r", "cc_g", 1)

        def p3_A(i):
            blk, c = divmod(i, NCH)
            if c == 0:
                ensure_load(4 + blk)
            proj_mms(4 + blk, c, pbank[i % 2], "p%d" % (i % 2))
            if c == NCH - 1:
                ensure_load(4 + blk + 1)

        def p3_V(i):
            blk, c = divmod(i, NCH)
            par = i % 2
            ip = pbank[par]
            bp = blk % 2
            vsl = vblk[:, bp * NCH * OB + c * OB: bp * NCH * OB + (c + 1) * OB]
            b.op("dve", lambda e, ip=ip, vsl=vsl: e.tensor_copy(vsl, ip[:, :]),
                 reads=("p%d" % par,), writes=("vb%d_%d" % (bp, c),))

        def p3_S(i):
            blk, c = divmod(i, NCH)
            sp = pbank[2]
            mm(sp[:, :], sel8[:, c * 128:(c + 1) * 128],
               rows8[0:8, blk * OB:(blk + 1) * OB], True, True,
               reads=("rows8", "consts"), writes=("p2",))

        def p3_K(i):
            blk, c = divmod(i, NCH)
            par = i % 2
            kd = kdst[:, par * OB:(par + 1) * OB]
            kbsl = kb[:, c * 2048 + blk * OB: c * 2048 + (blk + 1) * OB]
            sp = pbank[2]
            b.op("dve", lambda e, kd=kd, kbsl=kbsl, sp=sp: e.tensor_tensor(
                kd, kbsl, sp[:, :], ALU.mult),
                reads=("kbr", "p2"), writes=("kd%d" % par,))

        def p3_W(i):
            blk, c = divmod(i, NCH)
            par = i % 2
            bp = blk % 2
            kd = kdst[:, par * OB:(par + 1) * OB]
            vsl = vblk[:, bp * NCH * OB + c * OB: bp * NCH * OB + (c + 1) * OB]
            for hl in range(NHB):
                hg = blk * 4 + hl
                hup = pbank[3]
                mm(hup[:, 0:128], kd[:, hl * 128:(hl + 1) * 128],
                   vsl[:, hl * 128:(hl + 1) * 128], True, True,
                   reads=("kd%d" % par, "vb%d_%d" % (bp, c)), writes=("p3",))
                hsl = hsb[:, hg * 128:(hg + 1) * 128]
                ecsl = ecols[:, (c * 16 + hg) * 2 + 1: (c * 16 + hg) * 2 + 2]
                b.op("dve", lambda e, hsl=hsl, ecsl=ecsl, hup=hup:
                     e.scalar_tensor_tensor(hsl, hsl, ecsl, hup[:, 0:128],
                                            ALU.mult, ALU.add),
                     reads=("H%d" % blk, "ecols", "p3"), writes=("H%d" % blk,))
            if c == NCH - 1:
                vfull = vblk[:, bp * NCH * OB:(bp + 1) * NCH * OB].rearrange(
                    "p (cc d) -> p cc d", d=OB)
                dma("syq", vD[:, blk * OB:(blk + 1) * OB].rearrange(
                    "(cc p) d -> p cc d", p=128), vfull,
                    reads=tuple("vb%d_%d" % (bp, cc) for cc in range(NCH)),
                    writes=("vDram%d" % blk,))
                dma("syq",
                    gin[blk * 512:(blk + 1) * 512, :].rearrange(
                        "(h d) v -> d h v", d=128),
                    hsb[:, blk * 512:(blk + 1) * 512].rearrange(
                        "p (h v) -> p h v", v=128),
                    reads=("H%d" % blk,), writes=("ginb%d" % blk,))
                emit_coll("AllGather", [gin[blk * 512:(blk + 1) * 512, :].opt()],
                          [goutb[blk].opt()], "cc_h%d" % blk,
                          waits=[("syq", b.cnt["syq"])])
                b.mark_write("goutr%d" % blk, "cc_h%d" % blk, 1)

        pipeline([(0, p3_A), (1, p3_V), (2, p3_K), (1, p3_S), (2, p3_W)],
                 NBLK * NCH)
        ensure_load(9)
        if _DBG:
            dump_src["dvD"] = (vD, "vDram3")
            dump_src["dhsb"] = (hsb[:, :], "H3")
        b.run_block()
        if _MAXPH <= 4:
            do_dump()
            return nc

        for blk in range(4):
            b.op("dve", lambda e, blk=blk: e.memset(
                hsb[:, blk * 512:(blk + 1) * 512], 0.0),
                reads=("ginb%d" % blk,), writes=("H%d" % blk,))
        dma("syq", g2all.rearrange("p (c h) -> p c h", h=16),
            gout2.rearrange("(c d) h -> d c h", d=128),
            reads=("gout2r",), writes=("g2all",))
        b.op("dve", lambda e: e.tensor_copy(sall[:, 0:16], g2all[:, 0:16]),
             reads=("g2all",), writes=("sall",))
        for p in range(1, 8):
            b.op("dve", lambda e, p=p: e.tensor_tensor(
                sall[:, p * 16:(p + 1) * 16], sall[:, (p - 1) * 16:p * 16],
                g2all[:, p * 16:(p + 1) * 16], ALU.add),
                reads=("sall", "g2all"), writes=("sall",))
        b.op("dve", lambda e: e.memset(smy[:, :], 0.0), writes=("smy",))
        for p in range(8):
            b.op("dve", lambda e, p=p: e.scalar_tensor_tensor(
                smy[:, :], g2all[:, p * 16:(p + 1) * 16], pltr[:, p:p + 1],
                smy[:, :], ALU.mult, ALU.add),
                reads=("g2all", "consts", "smy"), writes=("smy",))
        for p in range(8):
            b.op("dve", lambda e, p=p: e.tensor_tensor(
                tdif[:, :], smy[:, :], sall[:, p * 16:(p + 1) * 16],
                ALU.subtract), reads=("smy", "sall"), writes=("tdif",))
            b.op("dve", lambda e, p=p: e.tensor_single_scalar(
                tdif[:, :], tdif[:, :], pltr[:, p:p + 1], ALU.mult),
                reads=("tdif", "consts"), writes=("tdif",))
            b.op("act", lambda e, p=p: e.activation(
                sclall[:, p * 16:(p + 1) * 16], tdif[:, :], AF.Exp),
                reads=("tdif",), writes=("sclall",))
            b.op("dve", lambda e, p=p: e.tensor_single_scalar(
                sclall[:, p * 16:(p + 1) * 16],
                sclall[:, p * 16:(p + 1) * 16], pltr[:, p:p + 1], ALU.mult),
                reads=("sclall", "consts"), writes=("sclall",))

        def emit_hin(blk):
            for p in range(8):
                pp = p % 2
                dma("syq", hp2[:, pp * 512:(pp + 1) * 512].rearrange(
                    "p (h v) -> p h v", v=128),
                    goutb[blk][p * 512:(p + 1) * 512, :].rearrange(
                        "(h d) v -> d h v", d=128),
                    reads=("goutr%d" % blk,), writes=("hpb%d" % pp,))
                for hl in range(4):
                    hg = blk * 4 + hl
                    hsl = hsb[:, hg * 128:(hg + 1) * 128]
                    b.op("dve", lambda e, hsl=hsl, hl=hl, pp=pp, p=p, hg=hg:
                         e.scalar_tensor_tensor(
                             hsl,
                             hp2[:, pp * 512 + hl * 128:
                                 pp * 512 + (hl + 1) * 128],
                             sclall[:, p * 16 + hg: p * 16 + hg + 1],
                             hsl, ALU.mult, ALU.add),
                         reads=("hpb%d" % pp, "sclall", "H%d" % blk),
                         writes=("H%d" % blk,))

        def p4_load(blk):
            bp = blk % 2
            dma("syq", eblk[:, bp * NCH * OB:(bp + 1) * NCH * OB].rearrange(
                "p (cc d) -> p cc d", d=OB),
                eD[:, blk * OB:(blk + 1) * OB].rearrange(
                    "(cc p) d -> p cc d", p=128),
                reads=("eDram",), writes=("eblk%d" % bp,))
            dma("syq", vblk[:, bp * NCH * OB:(bp + 1) * NCH * OB].rearrange(
                "p (cc d) -> p cc d", d=OB),
                vD[:, blk * OB:(blk + 1) * OB].rearrange(
                    "(cc p) d -> p cc d", p=128),
                reads=("vDram%d" % blk,), writes=("vbl%d" % bp,))

        p4_load(0)
        p4_load(1)

        def p4_A(i):
            blk, c = divmod(i, NCH)
            if c == 0:
                ensure_load(8 + blk)
                emit_hin(blk)
            proj_mms(8 + blk, c, pbank[i % 2], "p%d" % (i % 2))
            if c == NCH - 1:
                ensure_load(8 + blk + 1)

        def p4_S(i):
            par = i % 2
            qp = pbank[par]
            qs = qst[:, par * OB:(par + 1) * OB]
            b.op("act", lambda e, qp=qp, qs=qs: e.activation(
                qs, qp[:, :], AF.Silu),
                reads=("p%d" % par,), writes=("qs%d" % par,))

        def p4_Q(i):
            blk, c = divmod(i, NCH)
            par = i % 2
            bp = blk % 2
            qs = qst[:, par * OB:(par + 1) * OB]
            qbs = qbst[:, par * OB:(par + 1) * OB]
            eb = eblk[:, bp * NCH * OB + c * OB: bp * NCH * OB + (c + 1) * OB]
            b.op("dve", lambda e, qbs=qbs, qs=qs, eb=eb: e.tensor_tensor(
                qbs, qs, eb, ALU.mult),
                reads=("qs%d" % par, "eblk%d" % bp), writes=("qb%d" % par,))

        def p4_SB(i):
            blk, c = divmod(i, NCH)
            sp = pbank[6]
            mm(sp[:, :], sel8[:, c * 128:(c + 1) * 128],
               rows8[0:8, blk * OB:(blk + 1) * OB], True, True,
               reads=("rows8", "consts"), writes=("p6",))

        def p4_K(i):
            blk, c = divmod(i, NCH)
            par = i % 2
            kd = kdst[:, par * OB:(par + 1) * OB]
            kbsl = kb[:, c * 2048 + blk * OB: c * 2048 + (blk + 1) * OB]
            sp = pbank[6]
            b.op("dve", lambda e, kd=kd, kbsl=kbsl, sp=sp: e.tensor_tensor(
                kd, kbsl, sp[:, :], ALU.mult),
                reads=("kbr", "p6"), writes=("kd%d" % par,))

        def p4_W(i):
            blk, c = divmod(i, NCH)
            par = i % 2
            bp = blk % 2
            qbs = qbst[:, par * OB:(par + 1) * OB]
            kbsl = kb[:, c * 2048 + blk * OB: c * 2048 + (blk + 1) * OB]
            kd = kdst[:, par * OB:(par + 1) * OB]
            for hl in range(NHB):
                hg = blk * 4 + hl
                hpar = hl % 2
                vsl = vblk[:, bp * NCH * OB + c * OB + hl * 128:
                           bp * NCH * OB + c * OB + (hl + 1) * 128]
                qT = qbT[:, hpar * 128:(hpar + 1) * 128]
                kT = kbT[:, hpar * 128:(hpar + 1) * 128]
                am = asb[:, hpar * 128:(hpar + 1) * 128]
                hp = hpsb[:, hpar * 128:(hpar + 1) * 128]
                hsl = hsb[:, hg * 128:(hg + 1) * 128]
                ecg = ecols[:, (c * 16 + hg) * 2: (c * 16 + hg) * 2 + 1]
                ect = ecols[:, (c * 16 + hg) * 2 + 1: (c * 16 + hg) * 2 + 2]
                mm(pbank[2][:, 0:128], qbs[:, hl * 128:(hl + 1) * 128],
                   ident[:, :], True, True,
                   reads=("qb%d" % par,), writes=("p2",))
                mm(pbank[3][:, 0:128], kbsl[:, hl * 128:(hl + 1) * 128],
                   ident[:, :], True, True, reads=("kbr",), writes=("p3",))
                b.op("dve", lambda e, qT=qT: e.tensor_copy(
                    qT, pbank[2][:, 0:128]), reads=("p2",),
                    writes=("qT%d" % hpar,))
                b.op("dve", lambda e, kT=kT: e.tensor_copy(
                    kT, pbank[3][:, 0:128]), reads=("p3",),
                    writes=("kT%d" % hpar,))
                b.op("dve", lambda e, hp=hp, hsl=hsl, ecg=ecg:
                     e.tensor_single_scalar(hp, hsl, ecg, ALU.mult),
                     reads=("H%d" % blk, "ecols"), writes=("hpp%d" % hpar,))
                mm(pbank[4][:, 0:128], kT, qT, True, True,
                   reads=("kT%d" % hpar, "qT%d" % hpar), writes=("p4",))
                b.op("dve", lambda e, am=am: e.select(
                    am, masku[:, :], pbank[4][:, 0:128], zbf[:, :]),
                    reads=("p4", "consts", "zbf"), writes=("am%d" % hpar,))
                mm(pbank[5][:, 0:128], vsl, am, True, False,
                   reads=("am%d" % hpar, "vbl%d" % bp), writes=("p5",))
                mm(pbank[5][:, 0:128], hp, qT, False, True,
                   reads=("hpp%d" % hpar, "qT%d" % hpar), writes=("p5",))
                mm(pbank[7][:, 0:128], kd[:, hl * 128:(hl + 1) * 128], vsl,
                   True, True,
                   reads=("kd%d" % par, "vbl%d" % bp), writes=("p7",))
                osl = osb[:, hg * 1024 + c * 128: hg * 1024 + (c + 1) * 128]
                b.op("dve", lambda e, osl=osl: e.tensor_copy(
                    osl, pbank[5][:, 0:128]), reads=("p5", "p7"),
                    writes=("osbr",))
                b.op("dve", lambda e, hsl=hsl, ect=ect: e.scalar_tensor_tensor(
                    hsl, hsl, ect, pbank[7][:, 0:128], ALU.mult, ALU.add),
                    reads=("H%d" % blk, "ecols", "p7"), writes=("H%d" % blk,))
            if c == NCH - 1 and blk + 2 < NBLK:
                p4_load(blk + 2)

        pipeline([(0, p4_A), (1, p4_S), (2, p4_Q), (2, p4_K), (1, p4_SB),
                  (3, p4_W)], NBLK * NCH)
        if _DBG:
            dump_src["dosb"] = (osb[:, :], "osbr")
            dump_src["dzbf"] = (zbf[:, :], "zbf")
        issue_wload(12)
        issue_wload(13)
        b.run_block()
        if _MAXPH <= 6:
            do_dump()
            return nc

        es2b.close()
        es2.close()
        es3 = contextlib.ExitStack()
        def sb3(name, shape, dt):
            return es3.enter_context(nc.sbuf_tensor(name, shape, dt, side="right"))
        x1 = sb3("x1b", [128, NCH * 2048], F32)
        biasB = sb3("biasB", [128, 2048], F32)
        es3a = contextlib.ExitStack()
        def sb3a(name, shape, dt):
            return es3a.enter_context(nc.sbuf_tensor(name, shape, dt, side="right"))
        xts = sb3a("xts2", [128, 2048], F32)
        o2st = sb3a("o2st", [128, 1024], BF16)
        rstdrow = sb3a("rstdrow", [1, 1024], F32)
        rhirow = sb3a("rhirow", [1, 1024], BF16)
        rlorow = sb3a("rlorow", [1, 1024], BF16)
        rstdB = sb3a("rstdB", [128, 1024], F32)

        for h in range(NH):
            o2 = o2st[:, 0:1024]
            b.op("act", lambda e, o2=o2, h=h: e.activation(
                o2, osb[:, h * 1024:(h + 1) * 1024], AF.Square),
                reads=("osbr",), writes=("o2s",))
            for half in range(2):
                pk = pbank[half]
                mm(pk[0:1, :], onesc[:, :],
                   o2[:, half * 512:(half + 1) * 512], h == 0, h == NH - 1,
                   reads=("o2s", "consts"), writes=("p%d" % half,))
        for half in range(2):
            pk = pbank[half]
            rr = rstdrow[0:1, half * 512:(half + 1) * 512]
            b.op("act", lambda e, rr=rr, pk=pk: e.activation(
                rr, pk[0:1, :], AF.Ln, bias=EPS_RMS, scale=1.0 / 2048.0),
                reads=("p%d" % half,), writes=("rsr",))
            b.op("act", lambda e, rr=rr: e.activation(
                rr, rr, AF.Exp, scale=-0.5), reads=("rsr",), writes=("rsr",))
        b.op("dve", lambda e: e.tensor_copy(rhirow[0:1, :], rstdrow[0:1, :]),
             reads=("rsr",), writes=("rhi",))
        b.op("dve", lambda e: e.tensor_tensor(
            rlorow[0:1, :], rstdrow[0:1, :], rhirow[0:1, :], ALU.subtract),
            reads=("rsr", "rhi"), writes=("rlo",))
        for half in range(2):
            pk = pbank[2 + half]
            pr = "p%d" % (2 + half)
            mm(pk[:, :], ones1[:, :], rhirow[0:1, half * 512:(half + 1) * 512],
               True, False, reads=("rhi", "consts"), writes=(pr,))
            mm(pk[:, :], ones1[:, :], rlorow[0:1, half * 512:(half + 1) * 512],
               False, True, reads=("rlo",), writes=(pr,))
            b.op("dve", lambda e, pk=pk, half=half: e.tensor_copy(
                rstdB[:, half * 512:(half + 1) * 512], pk[:, :]),
                reads=(pr,), writes=("rstdB",))
        for h in range(NH):
            osl = osb[:, h * 1024:(h + 1) * 1024]
            b.op("dve", lambda e, osl=osl: e.tensor_tensor(
                osl, osl, rstdB[:, :], ALU.mult),
                reads=("osbr", "rstdB"), writes=("osbr",))
            b.op("dve", lambda e, osl=osl, h=h: e.tensor_single_scalar(
                osl, osl, gwcol[:, h:h + 1], ALU.mult),
                reads=("osbr", "consts"), writes=("osbr",))
        for blk in range(NBLK):
            idx = 12 + blk
            ensure_load(12 + blk)
            for t in range(NCH):
                j = blk * NCH + t
                par = j % 2
                pw = pbank[4 + par]
                pres = "p%d" % (4 + par)
                for r in range(HCN):
                    mm(pw[:, :],
                       osb[:, r * 1024 + t * 128: r * 1024 + (t + 1) * 128],
                       wsl(idx, r), r == 0, r == HCN - 1,
                       reads=("wslot%d" % (idx % 2), "osbr"), writes=(pres,))
                x1sl = x1[:, t * 2048 + blk * OB: t * 2048 + (blk + 1) * OB]
                b.op("dve", lambda e, x1sl=x1sl, pw=pw: e.tensor_copy(
                    x1sl, pw[:, :]), reads=(pres,), writes=("x1_%d" % t,))
            ensure_load(12 + blk + 1)
        for t in range(NCH):
            xt = xts[:, 0:2048]
            dma("syq", xt, xin[t * 128:(t + 1) * 128, :], writes=("xtsr",))
            x1sl = x1[:, t * 2048:(t + 1) * 2048]
            b.op("dve", lambda e, x1sl=x1sl, xt=xt: e.tensor_tensor(
                x1sl, x1sl, xt, ALU.add),
                reads=("x1_%d" % t, "xtsr"), writes=("x1_%d" % t,))
        if _DBG:
            dump_src["dx1"] = (x1[:, :], "x1_7")
        ensure_load(17)
        b.run_block()
        if _MAXPH <= 7:
            do_dump()
            return nc

        es3a.close()
        es3b0 = contextlib.ExitStack()
        def sb3b0(name, shape, dt):
            return es3b0.enter_context(nc.sbuf_tensor(name, shape, dt, side="right"))
        biasrow = sb3b0("brb", [1, 2048], F32)
        bhirow = sb3b0("bhirow", [1, 2048], BF16)
        blorow = sb3b0("blorow", [1, 2048], BF16)
        dma("syq", biasrow[:, :], biasd, writes=("biasrow",))

        b.op("dve", lambda e: e.tensor_copy(bhirow[0:1, :], biasrow[0:1, :]),
             reads=("biasrow",), writes=("bhi",))
        b.op("dve", lambda e: e.tensor_tensor(
            blorow[0:1, :], biasrow[0:1, :], bhirow[0:1, :], ALU.subtract),
            reads=("biasrow", "bhi"), writes=("blo",))
        for q in range(4):
            pk = pbank[q % 2]
            pr = "p%d" % (q % 2)
            mm(pk[:, :], ones1[:, :], bhirow[0:1, q * 512:(q + 1) * 512],
               True, False, reads=("bhi", "consts"), writes=(pr,))
            mm(pk[:, :], ones1[:, :], blorow[0:1, q * 512:(q + 1) * 512],
               False, True, reads=("blo",), writes=(pr,))
            b.op("dve", lambda e, pk=pk, q=q: e.tensor_copy(
                biasB[:, q * 512:(q + 1) * 512], pk[:, :]),
                reads=(pr,), writes=("biasB",))
        b.run_block()
        if _MAXPH <= 8:
            do_dump()
            return nc
        es3b0.close()
        es3b = contextlib.ExitStack()
        def sb3b(name, shape, dt):
            return es3b.enter_context(nc.sbuf_tensor(name, shape, dt, side="right"))
        xsq = sb3b("xsq2", [128, 2048], F32)
        ltst = sb3b("ltst2", [128, 2 * 2048], BF16)
        outst = sb3b("outst", [128, 2 * OB], F32)

        for t in range(NCH):
            emit_ln(t, None, "lnr")
        b.run_block()
        if _MAXPH <= 9:
            do_dump()
            return nc

        for blk in range(NBLK):
            idx = 16 + blk
            ensure_load(16 + blk)
            for t in range(NCH):
                j = blk * NCH + t
                par = j % 2
                pf = pbank[2 + par]
                pres = "p%d" % (2 + par)
                for ic in range(HCN):
                    mm(pf[:, :],
                       ln[:, ic * 1024 + t * 128: ic * 1024 + (t + 1) * 128],
                       wsl(idx, ic), ic == 0, ic == HCN - 1,
                       reads=("wslot%d" % (idx % 2), "lnr%d" % t), writes=(pres,))
                ost = outst[:, par * OB:(par + 1) * OB]
                b.op("dve", lambda e, ost=ost, pf=pf, blk=blk:
                     e.scalar_tensor_tensor(
                         ost, pf[:, :], 1.0,
                         biasB[:, blk * OB:(blk + 1) * OB], ALU.mult, ALU.add),
                     reads=(pres, "biasB"), writes=("ost%d" % par,))
                b.op("dve", lambda e, ost=ost, t=t, blk=blk: e.tensor_tensor(
                    ost, ost,
                    x1[:, t * 2048 + blk * OB: t * 2048 + (blk + 1) * OB],
                    ALU.add),
                    reads=("ost%d" % par, "x1_%d" % t), writes=("ost%d" % par,))
                dma("syq",
                    outd[t * 128:(t + 1) * 128, blk * OB:(blk + 1) * OB],
                    ost, reads=("ost%d" % par,))
            ensure_load(16 + blk + 1)
        vfin = b.cnt["syq"]
        b.prog["syq"].append(
            lambda e, s=b.sems["syq"], v=vfin: e.wait_ge(s, v))
        b.run_block()
        if _MAXPH <= 99:
            do_dump()
        es3b.close()
        es3.close()

    return nc


_NC_CACHE = {}
_LAST_RES = None


def _get_nc():
    if "nc" not in _NC_CACHE:
        _NC_CACHE["nc"] = build_nc()
    return _NC_CACHE["nc"]


def kernel(L0_attn_x, L0_attn_Wq, L0_attn_Wf, L0_attn_Wi, L0_attn_Wo,
           L0_attn_g_weight, L0_ffn_weight, L0_ffn_bias):
    import ml_dtypes
    BFD = ml_dtypes.bfloat16
    x = np.ascontiguousarray(
        np.asarray(L0_attn_x, dtype=np.float32).reshape(8192, 2048))
    Wq = np.ascontiguousarray(np.asarray(L0_attn_Wq, dtype=np.float32))
    Wf = np.ascontiguousarray(np.asarray(L0_attn_Wf, dtype=np.float32))
    Wi = np.ascontiguousarray(np.asarray(L0_attn_Wi, dtype=np.float32))
    Wo = np.ascontiguousarray(np.asarray(L0_attn_Wo, dtype=np.float32))
    gw = np.asarray(L0_attn_g_weight, dtype=np.float32)
    Wn = np.ascontiguousarray(np.asarray(L0_ffn_weight, dtype=np.float32))
    bias = np.asarray(L0_ffn_bias, dtype=np.float32)

    ident = np.eye(128, dtype=np.float32).astype(BFD)
    tau = np.arange(128)[:, None]
    tcol = np.arange(128)[None, :]
    lcm = ((tau <= tcol).astype(np.float32)
           - ((tau <= 63).astype(np.float32)
              * np.ones((1, 128), np.float32))).astype(BFD)
    u2m = np.concatenate(
        [(np.arange(128) <= 63).astype(np.float32)[:, None],
         np.ones((128, 1), np.float32)], 1).astype(BFD)
    u64 = (np.arange(128) >= 64).astype(np.float32)[:, None].astype(BFD)
    masku = (tau <= tcol).astype(np.uint8)
    ones1 = np.ones((1, 128), np.float32).astype(BFD)
    onesc = np.ones((128, 1), np.float32).astype(BFD)
    sel8 = np.zeros((8, 1024), np.float32)
    for c in range(8):
        sel8[c, c * 128:(c + 1) * 128] = 1.0
    sel8 = sel8.astype(BFD)
    gwcol = np.ascontiguousarray(gw.reshape(16, 128).T)
    biasr = np.ascontiguousarray(bias.reshape(1, 2048))

    in_maps = []
    for r in range(NCORE):
        pm = np.zeros((128, 8), np.float32)
        pidx = np.zeros((1, 8), np.int32)
        for j in range(1, 8):
            p = r - j
            if p >= 0:
                pm[:, j] = 1.0
                pidx[0, j] = p
        in_maps.append(dict(
            xin=np.ascontiguousarray(x[r * T:(r + 1) * T]),
            wf=Wf, wi=Wi, wq=Wq, wo=Wo,
            wns=np.ascontiguousarray(Wn[r * 256:(r + 1) * 256]),
            biasr=biasr, gwcol=gwcol,
            identin=ident, lcin=lcm, u2in=u2m, u64in=u64,
            maskin=masku, ones1in=ones1, onescin=onesc, sel8in=sel8,
            pmaskin=pm, pidxin=pidx,
            pltrin=np.tile((np.arange(8) < r).astype(np.float32)[None, :],
                           (128, 1)),
            cstin=np.tile(np.array([[1.0, LOG_DKSC, EPS_LN, 0.0]],
                                   np.float32), (128, 1)),
        ))
    nc = _get_nc()
    res = run_bass_kernel_spmd(nc, in_maps, core_ids=list(range(NCORE)))
    global _LAST_RES
    _LAST_RES = res.results
    out = np.concatenate([res.results[r]["out"] for r in range(NCORE)], axis=0)
    return np.ascontiguousarray(out.reshape(1, 8192, 2048), dtype=np.float32)


if __name__ == "__main__":
    nc = build_nc()
    n = sum(1 for _ in nc.all_instructions())
    print("built ok, instructions:", n)
